# version 11
# baseline (speedup 1.0000x reference)
import numpy as np
import ml_dtypes
import time
from contextlib import ExitStack

import concourse.bass as bass
import concourse.mybir as mybir
from concourse.bass_types import DynSlice
from concourse.bass_utils import run_bass_kernel_spmd

F32 = mybir.dt.float32
BF16 = mybir.dt.bfloat16
I32 = mybir.dt.int32
AF = mybir.ActivationFunctionType
ALU = mybir.AluOpType

B, S, D = 2, 8192, 1024
L = 4
DI = 2048
NST = 64
H, DH = 16, 64
DF = 4096
NEXP = 8
TC = 4096
TT = 512
NTT = TC // TT
NCH = D // 128
EPS = 1e-5
SCALE = 1.0 / 8.0
CSHIFT = 20.0
NEP = 2 * L

ENGS = ("sync", "gpsimd", "tensor", "scalar", "vector")


class Sched:
    def __init__(self):
        self.prog = {e: [] for e in ENGS}
        self.cnt = {}

    def sem(self, name):
        self.cnt.setdefault(name, 0)
        return name

    def raw(self, eng, fn):
        self.prog[eng].append(fn)

    def op(self, eng, attr, *a, incs=(), **k):
        for sname, v in incs:
            self.sem(sname)
            self.cnt[sname] += v

        def fn(e, sems, attr=attr, a=a, k=k, incs=incs):
            inst = getattr(e, attr)(*a, **k)
            for sname, v in incs:
                inst.then_inc(sems[sname], v)
        self.prog[eng].append(fn)

    def wait(self, eng, sname, val=None):
        self.sem(sname)
        v = self.cnt[sname] if val is None else val

        def fn(e, sems, sname=sname, v=v):
            e.wait_ge(sems[sname], v)
        self.prog[eng].append(fn)
        return v

    def barrier(self, engs=ENGS):
        for e in engs:
            self.sem("ph_" + e)
        for e in engs:
            self.op(e, "nop", incs=[("ph_" + e, 1)])
        targets = {e: self.cnt["ph_" + e] for e in engs}
        for e in engs:
            for o in engs:
                if o != e:
                    self.wait(e, "ph_" + o, targets[o])


def build_program(upto=None):
    nc = bass.Bass("TRN2", target_bir_lowering=False, debug=False)
    s = Sched()

    def din(name, shape, dtype=F32):
        return nc.dram_tensor(name, shape, dtype, kind="ExternalInput").ap()

    xT = din("xT", [D, TC])
    w_in = din("w_in", [L, D, 2 * DI])
    w_dt = din("w_dt", [L, DI, D])
    w_xp = din("w_xp", [L, DI, NST])
    w_out = din("w_out", [L, DI, D])
    w_q = din("w_q", [L, D, D])
    w_k = din("w_k", [L, D, D])
    w_v = din("w_v", [L, D, H * 65])
    w_ao = din("w_ao", [L, D, D])
    w_f1 = din("w_f1", [L, D, DF])
    w_f2 = din("w_f2", [L, DF, D])
    bcols = din("bcols", [128, L * 160])
    bxp = din("bxp", [64, L])
    alog = din("alog", [64, L])
    bvrep = din("bvrep", [L, 128, H * 65])
    maskS = din("maskS", [64, 1])
    w_r = din("w_r", [D, NEXP])
    br_rep = din("br_rep", [128, NEXP])
    w1 = din("w1", [NEXP, D, DF])
    b1c = din("b1c", [128, NEXP * 32])
    w2 = din("w2", [NEXP, DF, D])
    b2r = din("b2r", [NEXP, D])
    ones_bf = din("ones_bf", [128, 128], BF16)
    ones8f = din("ones8f", [128, NEXP])
    basis = din("basis", [NEXP, NEXP * 128], BF16)
    identf = din("identf", [128, 128])
    cst = din("cst", [128, 8])
    flagrows = din("flagrows", [1, NEP], I32)

    out = nc.dram_tensor("out", [D, TC], F32, kind="ExternalOutput").ap()

    sp_xi = nc.dram_tensor("sp_xi", [128, 16 * TC], BF16).ap()
    sp_q = nc.dram_tensor("sp_q", [D, TC], BF16).ap()
    sp_o = nc.dram_tensor("sp_o", [128, NCH * TC], BF16).ap()
    lc_k = nc.dram_tensor("lc_k", [D, TC], BF16).ap()
    lc_v = nc.dram_tensor("lc_v", [NCH * TC, 130], BF16).ap()
    sp_x = nc.dram_tensor("sp_x", [128, NCH * TC], BF16).ap()
    sp_x2 = nc.dram_tensor("sp_x2", [128, NCH * TC], BF16).ap()
    sp_z2 = nc.dram_tensor("sp_z2", [128, NCH * TC], BF16).ap()

    sh_k = [nc.dram_tensor(f"sh_k{l}", [2 * D, TC], BF16, addr_space="Shared").ap()
            for l in range(L)]
    sh_v = [nc.dram_tensor(f"sh_v{l}", [2 * NCH * TC, 130], BF16, addr_space="Shared").ap()
            for l in range(L)]
    sh_sc = [nc.dram_tensor(f"sh_sc{l}", [2 * 64, 2], F32, addr_space="Shared").ap()
             for l in range(L)]
    sh_fl = nc.dram_tensor("sh_fl", [NEP, 2], I32, addr_space="Shared").ap()

    with ExitStack() as ctx:
        def sb(name, shape, dt):
            return ctx.enter_context(nc.sbuf_tensor(name, shape, dt))

        big0 = sb("big0", [128, NCH, TC], BF16)
        big1 = sb("big1", [128, 33024], BF16)
        big2 = sb("big2", [128, 16384], BF16)
        big3 = sb("big3", [128, 8192], BF16)
        f32p = sb("f32p", [128, 2048], F32)
        ptx = sb("ptx", [128, 2, 1024], BF16)
        rcpb = sb("rcpb", [1, 1024], BF16)
        sc_a = sb("sc_a", [64, TT], F32)
        sc_t = sb("sc_t", [64, TT], F32)
        sc_u = sb("sc_u", [64, TT], F32)
        sc_z = sb("sc_z", [64, TT], F32)
        sc_st = sb("sc_st", [128, 4], F32)
        scmr = sb("scmr", [128, 128], BF16)
        msk_s = sb("msk_s", [128, 1], F32)
        onesb = sb("onesb", [128, 128], BF16)
        ones8 = sb("ones8", [128, NEXP], F32)
        basis_s = sb("basis_s", [NEXP, NEXP * 128], BF16)
        ident_s = sb("ident_s", [128, 128], F32)
        cst_s = sb("cst_s", [128, 8], F32)
        bcols_s = sb("bcols_s", [128, 160], F32)
        bxp_s = sb("bxp_s", [64, L], F32)
        a_d = sb("a_d", [64, L], F32)
        bvrep_s = sb("bvrep_s", [128, H * 65], BF16)
        b1c_s = sb("b1c_s", [128, 32], F32)
        b2r_s = sb("b2r_s", [NEXP, D], BF16)
        brr_s = sb("brr_s", [128, NEXP], F32)
        wr_s = sb("wr_s", [128, NCH, NEXP], F32)
        fl_s = sb("fl_s", [1, NEP], I32)
        pf_s = sb("pf_s", [1, 2], I32)
        logit = sb("logit", [128, 32, NEXP], F32)

        def stg(sl):
            return f32p[:, sl * TT:(sl + 1) * TT]

        def mge(a, n):
            return f32p[:, 1024 + a * NEXP:1024 + a * NEXP + n]
        psA = ctx.enter_context(nc.psum_tensor("psA", [128, 1024], F32))
        psB = ctx.enter_context(nc.psum_tensor("psB", [128, 1024], F32))
        psC = ctx.enter_context(nc.psum_tensor("psC", [128, 1024], F32))
        psD = ctx.enter_context(nc.psum_tensor("psD", [128, 1024], F32))

        sems = {}
        for nm in (["ld", "wg", "mm", "ev", "dv", "sp", "xw", "pp", "ost",
                    "vv", "sv"]
                   + ["ph_" + e for e in ENGS]):
            sems[nm] = ctx.enter_context(nc.semaphore(nm))
            s.sem(nm)

        block = ctx.enter_context(nc.Block())

        par_cache = {}

        def get_par(e, key="par"):
            ck = (id(e), key)
            if ck not in par_cache:
                base = (id(e), "par")
                if base not in par_cache:
                    par_cache[base] = e.partition_id() % 2
                par = par_cache[base]
                if key == "parD":
                    par_cache[ck] = par * D
                elif key == "parTC":
                    par_cache[ck] = par * TC
                elif key == "par64":
                    par_cache[ck] = par * 64
                elif key == "parV":
                    par_cache[ck] = par * (NCH * TC)
            return par_cache[ck]

        poll_regs = {}

        def get_poll_regs(e):
            if not poll_regs:
                poll_regs["c"] = e.alloc_register("pollc")
                poll_regs["w"] = e.alloc_register("pollw")
                poll_regs["v0"] = e.alloc_register("pollv0")
                poll_regs["v1"] = e.alloc_register("pollv1")
                poll_regs["vm"] = e.alloc_register("pollvm")
                poll_regs["t0"] = e.alloc_register("pollt0")
                poll_regs["t1"] = e.alloc_register("pollt1")
                e.reg_mov(poll_regs["w"], 0)
            return poll_regs

        BC = {"b_in": (0, 32), "b_dt": (32, 8), "b_out": (40, 8), "b_q": (48, 8),
              "b_k": (56, 8), "b_ao": (64, 8), "g1": (72, 8), "be1": (80, 8),
              "g2": (88, 8), "be2": (96, 8), "b_f1": (104, 32), "b_f2": (136, 8),
              "Dp": (144, 8)}

        def bc(layer, name, chunk):
            off, n = BC[name]
            assert chunk < n
            return bcols_s[:, off + chunk:off + chunk + 1]

        CST_NEGC, CST_SCALE, CST_INVD, CST_EPS, CST_NHALF, CST_ONE = range(6)

        def cstc(i):
            return cst_s[:, i:i + 1]

        def emit_flag(ep):
            def fl(e, sems, ep=ep):
                e.dma_start(out=sh_fl[ep:ep + 1, DynSlice(get_par(e), 1)],
                            in_=fl_s[:, ep:ep + 1]).then_inc(sems["xw"], 16)
            s.raw("gpsimd", fl)
            s.cnt["xw"] += 16
            s.wait("gpsimd", "xw")

        def emit_poll(ep):
            def pl(e, sems, ep=ep):
                r = get_poll_regs(e)
                e.reg_mov(r["c"], 1)
                with e.While(r["c"]):
                    e.reg_add(r["w"], r["w"], 16)
                    e.dma_start(out=pf_s[:, 0:2],
                                in_=sh_fl[ep:ep + 1, :]).then_inc(sems["pp"], 16)
                    e.wait_ge(sems["pp"], r["w"])
                    e.load(r["v0"], pf_s[0:1, 0:1])
                    e.load(r["v1"], pf_s[0:1, 1:2])
                    e.load(r["vm"], fl_s[0:1, ep:ep + 1])
                    e.reg_sub(r["t0"], r["v0"], r["vm"])
                    e.reg_sub(r["t1"], r["v1"], r["vm"])
                    with e.If_eq(r["t0"], 0):
                        with e.If_eq(r["t1"], 0):
                            e.reg_mov(r["c"], 0)
            s.raw("gpsimd", pl)

        def preload():
            for dst, srca in [(onesb, ones_bf), (ones8, ones8f), (basis_s, basis),
                              (ident_s, identf), (cst_s, cst), (bxp_s, bxp),
                              (brr_s, br_rep), (fl_s, flagrows)]:
                s.op("sync", "dma_start", out=dst[:, :], in_=srca, incs=[("ld", 16)])
            s.op("sync", "dma_start", out=msk_s[64:128, :], in_=maskS, incs=[("ld", 16)])
            s.op("sync", "dma_start", out=wr_s[:, :, :],
                 in_=w_r.rearrange("(c p) e -> p c e", p=128), incs=[("ld", 16)])
            s.op("sync", "dma_start", out=a_d[:, :], in_=alog, incs=[("ld", 16)])
            s.op("gpsimd", "dma_start", out=b2r_s[:, :], in_=b2r, incs=[("wg", 16)])
            s.op("gpsimd", "dma_start", out=big0[:, :, :],
                 in_=xT.rearrange("(c p) t -> p c t", p=128), incs=[("wg", 16)])
            s.op("vector", "memset", sc_z[:, :], 0.0, incs=[("dv", 1)])
            s.op("vector", "memset", scmr[0:64, :], 1.0, incs=[("dv", 1)])
            s.wait("scalar", "ld")
            s.op("scalar", "activation", out=a_d[:, :], in_=a_d[:, :], func=AF.Exp,
                 incs=[("ev", 1)])
            s.wait("vector", "ev")
            s.op("vector", "tensor_scalar_mul", out=a_d[:, :], in0=a_d[:, :],
                 scalar1=-1.0 / D, incs=[("dv", 1)])
            s.barrier()

        def layernorm(layer, affine, router=False, spill_dst=None):
            for tt in range(NTT):
                t0 = tt * TT
                sq = big3[:, 0:8 * TT].rearrange("p (c t) -> p c t", t=TT)
                s.wait("tensor", "ev")
                for c in range(NCH):
                    s.op("tensor", "matmul", psC[:, 0:TT], onesb[:, :],
                         big0[:, c, t0:t0 + TT], start=(c == 0), stop=(c == NCH - 1),
                         incs=([("mm", 1)] if c == NCH - 1 else []))
                s.wait("scalar", "dv")
                for c in range(NCH):
                    s.op("scalar", "activation", out=sq[:, c, :],
                         in_=big0[:, c, t0:t0 + TT], func=AF.Square,
                         incs=[("ev", 1)] if c == NCH - 1 else [("sv", 1)])
                s.wait("tensor", "ev")
                s.wait("tensor", "sv", s.cnt["sv"])
                for c in range(NCH):
                    s.op("tensor", "matmul", psC[:, TT:2 * TT], onesb[:, :],
                         sq[:, c, :], start=(c == 0), stop=(c == NCH - 1),
                         incs=([("mm", 1)] if c == NCH - 1 else []))
                mu = f32p[:, 0:TT]
                var = f32p[:, TT:2 * TT]
                rstd = f32p[:, 2 * TT:3 * TT]
                tx = f32p[:, 3 * TT:4 * TT]
                if spill_dst is not None and tt > 0:
                    s.wait("vector", "sp")
                s.wait("vector", "mm")
                s.op("vector", "tensor_scalar_mul", out=mu, in0=psC[:, 0:TT],
                     scalar1=cstc(CST_INVD), incs=[("vv", 1)])
                s.op("vector", "tensor_scalar_mul", out=var, in0=psC[:, TT:2 * TT],
                     scalar1=cstc(CST_INVD), incs=[("dv", 1)])
                s.wait("vector", "dv")
                s.op("vector", "tensor_mul", out=tx, in0=mu, in1=mu, incs=[("vv", 1)])
                s.wait("vector", "vv")
                s.op("vector", "tensor_sub", out=var, in0=var, in1=tx,
                     incs=[("dv", 1)])
                s.wait("scalar", "dv")
                s.op("scalar", "activation", out=rstd, in_=var,
                     func=AF.Ln, bias=cstc(CST_EPS), incs=[("sv", 1)])
                s.wait("scalar", "sv")
                s.op("scalar", "activation", out=rstd, in_=rstd,
                     func=AF.Exp, scale=cstc(CST_NHALF), incs=[("ev", 2)])
                s.wait("vector", "ev")
                for c in range(NCH):
                    if router and c > 0:
                        s.wait("vector", "mm")
                    s.op("vector", "tensor_sub", out=tx,
                         in0=big0[:, c, t0:t0 + TT], in1=mu, incs=[("vv", 1)])
                    s.wait("vector", "vv")
                    s.op("vector", "tensor_mul", out=tx, in0=tx,
                         in1=rstd, incs=[("dv", 1)])
                    if router:
                        s.wait("tensor", "dv")
                        s.op("tensor", "matmul", psD[0:NEXP, 0:TT],
                             wr_s[:, c, :], tx[:, 0:TT],
                             start=(c == 0), stop=(c == NCH - 1),
                             incs=[("mm", 1)])
                    dst_ap = (big0[:, c, t0:t0 + TT] if spill_dst is None
                              else big3[:, 4096 + c * TT: 4096 + (c + 1) * TT])
                    s.wait("vector", "dv")
                    if affine is not None:
                        gn, bn = affine
                        s.op("vector", "tensor_scalar", out=dst_ap,
                             in0=tx, scalar1=bc(layer, gn, c),
                             scalar2=bc(layer, bn, c), op0=ALU.mult, op1=ALU.add,
                             incs=[("dv", 1)])
                    else:
                        if router:
                            s.wait("vector", "mm")
                        s.op("vector", "tensor_copy", out=dst_ap, in_=tx,
                             incs=[("dv", 1)])
                if router:
                    lgT = f32p[0:NEXP, 0:TT]
                    s.wait("vector", "mm")
                    s.op("vector", "tensor_copy", out=lgT, in_=psD[0:NEXP, 0:TT],
                         incs=[("dv", 1)])
                    for tg in range(TT // 128):
                        s.wait("tensor", "dv")
                        s.op("tensor", "transpose", psA[:, 0:NEXP],
                             lgT[:, tg * 128:(tg + 1) * 128], ident_s[0:NEXP, 0:NEXP],
                             incs=[("mm", 1)])
                        s.wait("vector", "mm")
                        s.op("vector", "tensor_add",
                             out=logit[:, tt * 4 + tg, :],
                             in0=psA[:, 0:NEXP],
                             in1=brr_s[:, :], incs=[("dv", 1)])
                if spill_dst is not None:
                    s.wait("gpsimd", "dv")
                    s.op("gpsimd", "dma_start",
                         out=spill_dst.rearrange("p (c t) -> p c t", t=TC)[:, :, t0:t0 + TT],
                         in_=big3[:, 4096:4096 + NCH * TT].rearrange("p (c t) -> p c t", t=TT),
                         incs=[("sp", 16)])
            s.barrier()

        def mamba_p1(layer, part="all"):
            winA = big1[:, 0:16384].rearrange("p (c o) -> p c o", o=DI)
            wdt = big1[:, 16384:32768].rearrange("p (c o) -> p c o", o=D)
            wxp = big3[:, 7168:8192].rearrange("p (c o) -> p c o", o=NST)
            s.op("gpsimd", "dma_start", out=winA[:, :, :],
                 in_=w_in[layer].rearrange("(c p) o -> p c o", p=128)[:, :, 0:DI],
                 incs=[("wg", 16)])
            s.op("gpsimd", "dma_start", out=wdt[:, :, :],
                 in_=w_dt[layer].rearrange("(c p) o -> p c o", p=128), incs=[("wg", 16)])
            s.op("gpsimd", "dma_start", out=wxp[:, :, :],
                 in_=w_xp[layer].rearrange("(c p) o -> p c o", p=128), incs=[("wg", 16)])
            s.op("sync", "dma_start", out=bcols_s[:, :],
                 in_=bcols[:, layer * 160:(layer + 1) * 160], incs=[("ld", 16)])
            s.op("vector", "memset", sc_st[0:64, 0:1], 0.0, incs=[("dv", 1)])
            s.op("vector", "memset", sc_st[0:64, 1:2], 1.0, incs=[("dv", 1)])
            s.wait("tensor", "wg")
            s.wait("vector", "ld")
            s.wait("scalar", "ld")
            xi = big2[:, 0:8192].rearrange("p (c t) -> p c t", t=TT)
            spb = big3[:, 0:TT]
            for tt in range(NTT):
                t0 = tt * TT
                for oc in range(16):
                    ph = psA[:, (oc % 2) * TT:(oc % 2 + 1) * TT]
                    if tt or oc >= 2:
                        s.wait("tensor", "ev")
                    for c in range(NCH):
                        s.op("tensor", "matmul", ph, winA[:, c, oc * 128:(oc + 1) * 128],
                             big0[:, c, t0:t0 + TT], start=(c == 0), stop=(c == NCH - 1),
                             incs=([("mm", 1)] if c == NCH - 1 else []))
                    s.wait("scalar", "mm")
                    if tt > 0 and oc == 0:
                        s.wait("scalar", "sp")
                    s.op("scalar", "activation", out=xi[:, oc, :], in_=ph,
                         func=AF.Silu, bias=bc(layer, "b_in", oc), incs=[("ev", 1)])
                s.wait("gpsimd", "ev")
                s.op("gpsimd", "dma_start",
                     out=sp_xi.rearrange("p (n t) -> p n t", t=16 * TT)[:, tt, :],
                     in_=big2[:, 0:8192], incs=[("sp", 16)])
                dv_tile_prev = s.cnt["dv"]
                dv_spb = {}
                for dc in range(NCH):
                    ph = psB[:, (dc % 2) * TT:(dc % 2 + 1) * TT]
                    if dc >= 2:
                        s.wait("tensor", "dv", dv_spb[dc - 2])
                    elif tt > 0:
                        s.wait("tensor", "dv", dv_tile_prev)
                    for kc in range(16):
                        s.op("tensor", "matmul", ph, wdt[:, kc, dc * 128:(dc + 1) * 128],
                             xi[:, kc, :], start=(kc == 0), stop=(kc == 15),
                             incs=([("mm", 1)] if kc == 15 else []))
                    s.wait("scalar", "mm")
                    s.op("scalar", "activation", out=f32p[:, 0:TT], in_=ph, func=AF.Abs,
                         incs=[("sv", 1)])
                    s.wait("scalar", "sv")
                    s.op("scalar", "activation", out=f32p[:, TT:2 * TT],
                         in_=f32p[:, 0:TT], func=AF.Exp, scale=-1.0, incs=[("sv", 1)])
                    s.wait("scalar", "sv")
                    s.op("scalar", "activation", out=f32p[:, TT:2 * TT],
                         in_=f32p[:, TT:2 * TT], func=AF.Ln, bias=cstc(CST_ONE),
                         incs=[("ev", 1)])
                    s.wait("vector", "ev")
                    s.op("vector", "tensor_scalar_max", out=f32p[:, 2 * TT:3 * TT],
                         in0=ph, scalar1=0.0, incs=[("vv", 1)])
                    s.wait("vector", "vv")
                    s.op("vector", "tensor_add", out=spb,
                         in0=f32p[:, 2 * TT:3 * TT], in1=f32p[:, TT:2 * TT],
                         incs=[("dv", 1)])
                    dv_spb[dc] = s.cnt["dv"]
                    s.wait("tensor", "dv")
                    s.op("tensor", "matmul", psC[0:64, 0:TT], onesb[:, 0:64], spb,
                         start=(dc == 0), stop=(dc == NCH - 1),
                         incs=([("mm", 1)] if dc == NCH - 1 else []))
                s.wait("tensor", "ev")
                for kc in range(16):
                    s.op("tensor", "matmul", psD[0:64, 0:TT], wxp[:, kc, :],
                         xi[:, kc, :], start=(kc == 0), stop=(kc == 15),
                         incs=([("mm", 1)] if kc == 15 else []))
                s.wait("scalar", "mm")
                s.wait("scalar", "dv")
                s.op("scalar", "activation", out=sc_a[:, :],
                     in_=psC[0:64, 0:TT], func=AF.Exp,
                     scale=a_d[:, layer:layer + 1])
                s.op("scalar", "nop", incs=[("ev", 2)])
                s.wait("vector", "ev")
                s.op("vector", "tensor_scalar_add", out=sc_u[:, :], in0=psD[0:64, 0:TT],
                     scalar1=bxp_s[:, layer:layer + 1], incs=[("dv", 1)])
                s.wait("vector", "dv")
                s.op("vector", "tensor_tensor_scan", out=sc_t[:, :],
                     data0=sc_a[:, :], data1=sc_u[:, :],
                     initial=sc_st[0:64, 0:1], op0=ALU.mult, op1=ALU.add,
                     incs=[("vv", 1)])
                s.wait("vector", "vv")
                s.op("vector", "tensor_copy", out=big3[0:64, 512 + t0:512 + t0 + TT],
                     in_=sc_t[:, :])
                s.op("vector", "tensor_copy", out=sc_st[0:64, 0:1],
                     in_=sc_t[:, TT - 1:TT], incs=[("vv", 1)])
                s.wait("vector", "vv")
                s.op("vector", "tensor_tensor_scan", out=sc_t[:, :],
                     data0=sc_a[:, :], data1=sc_z[:, :],
                     initial=sc_st[0:64, 1:2], op0=ALU.mult, op1=ALU.add,
                     incs=[("vv", 1)])
                s.wait("vector", "vv")
                s.op("vector", "tensor_copy", out=big3[64:128, 512 + t0:512 + t0 + TT],
                     in_=sc_t[:, :])
                s.op("vector", "tensor_copy", out=sc_st[0:64, 1:2],
                     in_=sc_t[:, TT - 1:TT], incs=[("dv", 1)])
                s.wait("vector", "dv")
            if part == "tiles":
                s.barrier()
                return
            s.wait("gpsimd", "dv")

            def wr_sc(e, sems, layer=layer):
                e.dma_start(out=sh_sc[layer][DynSlice(get_par(e, "par64"), 64), :],
                            in_=sc_st[0:64, 0:2]).then_inc(sems["xw"], 16)
            s.raw("gpsimd", wr_sc)
            s.cnt["xw"] += 16
            s.wait("gpsimd", "xw")
            emit_flag(2 * layer)
            emit_poll(2 * layer)
            s.op("gpsimd", "dma_start", out=f32p[64:128, 0:2],
                 in_=sh_sc[layer][0:64, :], incs=[("wg", 16)])
            s.wait("vector", "wg")
            s.op("vector", "tensor_mul", out=sc_st[64:128, 2:3], in0=f32p[64:128, 0:1],
                 in1=msk_s[64:128, :], incs=[("vv", 1)])
            s.wait("vector", "vv")
            s.op("vector", "tensor_scalar_mul", out=scmr[64:128, :], in0=onesb[64:128, :],
                 scalar1=sc_st[64:128, 2:3], incs=[("dv", 1)])
            s.barrier()

        def mamba_p2(layer, part="all"):
            winB = big1[:, 0:16384].rearrange("p (c o) -> p c o", o=DI)
            wout = big1[:, 16384:32768].rearrange("p (c o) -> p c o", o=D)
            s.op("gpsimd", "dma_start", out=winB[:, :, :],
                 in_=w_in[layer].rearrange("(c p) o -> p c o", p=128)[:, :, DI:2 * DI],
                 incs=[("wg", 16)])
            s.op("gpsimd", "dma_start", out=wout[:, :, :],
                 in_=w_out[layer].rearrange("(c p) o -> p c o", p=128), incs=[("wg", 16)])
            s.wait("tensor", "wg")
            sres = big2[:, 0:8192].rearrange("p (c t) -> p c t", t=TT)
            xi = big2[:, 8192:16384].rearrange("p (c t) -> p c t", t=TT)
            for tt in range(NTT):
                t0 = tt * TT
                s.wait("gpsimd", "mm")
                s.op("gpsimd", "dma_start", out=big2[:, 8192:16384],
                     in_=sp_xi.rearrange("p (n t) -> p n t", t=16 * TT)[:, tt, :],
                     incs=[("wg", 16)])
                for oc in range(16):
                    ph = psA[:, (oc % 2) * TT:(oc % 2 + 1) * TT]
                    if tt or oc >= 2:
                        s.wait("tensor", "ev")
                    for c in range(NCH):
                        s.op("tensor", "matmul", ph, winB[:, c, oc * 128:(oc + 1) * 128],
                             big0[:, c, t0:t0 + TT], start=(c == 0), stop=(c == NCH - 1),
                             incs=([("mm", 1)] if c == NCH - 1 else []))
                    s.wait("scalar", "mm")
                    s.op("scalar", "activation", out=sres[:, oc, :], in_=ph,
                         func=AF.Silu, bias=bc(layer, "b_in", 16 + oc), incs=[("ev", 1)])
                s.wait("tensor", "dv")
                s.op("tensor", "matmul", psC[:, 0:TT], scmr[:, :],
                     big3[:, 512 + t0:512 + t0 + TT], start=True, stop=True,
                     incs=[("mm", 1)])
                s.wait("vector", "ev")
                s.wait("vector", "mm")
                s.wait("vector", "wg")
                for oc in range(16):
                    s.op("vector", "tensor_mul", out=big3[:, 0:TT], in0=xi[:, oc, :],
                         in1=sres[:, oc, :], incs=[("vv", 1)])
                    s.wait("vector", "vv")
                    s.op("vector", "tensor_mul", out=xi[:, oc, :], in0=big3[:, 0:TT],
                         in1=psC[:, 0:TT], incs=[("dv", 1)])
                if part == "b":
                    continue
                s.wait("tensor", "dv")
                for dc in range(NCH):
                    ph = psB[:, (dc % 2) * TT:(dc % 2 + 1) * TT]
                    if tt or dc >= 2:
                        s.wait("tensor", "dv")
                    for kc in range(16):
                        s.op("tensor", "matmul", ph, wout[:, kc, dc * 128:(dc + 1) * 128],
                             xi[:, kc, :], start=(kc == 0), stop=(kc == 15),
                             incs=([("mm", 1)] if kc == 15 else []))
                    s.wait("vector", "mm")
                    s.op("vector", "tensor_scalar_add", out=f32p[:, 0:TT], in0=ph,
                         scalar1=bc(layer, "b_out", dc), incs=[("dv", 1)])
                    s.wait("vector", "dv")
                    s.op("vector", "scalar_tensor_tensor", out=f32p[:, TT:2 * TT],
                         in0=big0[:, dc, t0:t0 + TT], scalar=bc(layer, "Dp", dc),
                         in1=f32p[:, 0:TT], op0=ALU.mult, op1=ALU.add, incs=[("vv", 1)])
                    s.wait("vector", "vv")
                    s.op("vector", "tensor_add", out=big0[:, dc, t0:t0 + TT],
                         in0=f32p[:, TT:2 * TT], in1=big0[:, dc, t0:t0 + TT],
                         incs=[("dv", 1)])
            s.barrier()

        def qkv(layer):
            wq = big1[:, 0:8192].rearrange("p (c o) -> p c o", o=D)
            wk = big1[:, 8192:16384].rearrange("p (c o) -> p c o", o=D)
            wv = big1[:, 16384:16384 + 8 * 1040].rearrange("p (c o) -> p c o", o=H * 65)
            s.op("gpsimd", "dma_start", out=wq[:, :, :],
                 in_=w_q[layer].rearrange("(c p) o -> p c o", p=128), incs=[("wg", 16)])
            s.op("gpsimd", "dma_start", out=wk[:, :, :],
                 in_=w_k[layer].rearrange("(c p) o -> p c o", p=128), incs=[("wg", 16)])
            s.op("gpsimd", "dma_start", out=wv[:, :, :],
                 in_=w_v[layer].rearrange("(c p) o -> p c o", p=128), incs=[("wg", 16)])
            s.op("gpsimd", "dma_start", out=bvrep_s[:, :], in_=bvrep[layer],
                 incs=[("wg", 16)])
            s.wait("tensor", "wg")
            qs = big2[:, 0:4096].rearrange("p (c t) -> p c t", t=TT)
            ks = big2[:, 4096:8192].rearrange("p (c t) -> p c t", t=TT)
            for tt in range(NTT):
                t0 = tt * TT
                if tt > 0:
                    s.wait("tensor", "sp")
                    s.wait("tensor", "xw")
                for oc in range(NCH):
                    phq = psA[:, (oc % 2) * TT:(oc % 2 + 1) * TT]
                    phk = psB[:, (oc % 2) * TT:(oc % 2 + 1) * TT]
                    if tt or oc >= 2:
                        s.wait("tensor", "ev")
                    for c in range(NCH):
                        s.op("tensor", "matmul", phq, wq[:, c, oc * 128:(oc + 1) * 128],
                             big0[:, c, t0:t0 + TT], start=(c == 0), stop=(c == NCH - 1))
                    for c in range(NCH):
                        s.op("tensor", "matmul", phk, wk[:, c, oc * 128:(oc + 1) * 128],
                             big0[:, c, t0:t0 + TT], start=(c == 0), stop=(c == NCH - 1),
                             incs=([("mm", 1)] if c == NCH - 1 else []))
                    s.wait("vector", "mm")
                    s.op("vector", "tensor_scalar_add", out=qs[:, oc, :], in0=phq,
                         scalar1=bc(layer, "b_q", oc))
                    s.op("vector", "tensor_scalar_add", out=ks[:, oc, :], in0=phk,
                         scalar1=bc(layer, "b_k", oc), incs=[("ev", 2)])
                for mg in range(TT // 128):
                    m0 = t0 + mg * 128
                    s.wait("tensor", "dv")
                    for c in range(NCH):
                        s.op("tensor", "matmul", psC[:, 0:512],
                             big0[:, c, m0:m0 + 128], wv[:, c, 0:512],
                             start=(c == 0), stop=(c == NCH - 1))
                    for c in range(NCH):
                        s.op("tensor", "matmul", psC[:, 512:1024],
                             big0[:, c, m0:m0 + 128], wv[:, c, 512:1024],
                             start=(c == 0), stop=(c == NCH - 1))
                    for c in range(NCH):
                        s.op("tensor", "matmul", psD[:, 0:16],
                             big0[:, c, m0:m0 + 128], wv[:, c, 1024:1040],
                             start=(c == 0), stop=(c == NCH - 1),
                             incs=([("mm", 1)] if c == NCH - 1 else []))
                    s.wait("vector", "mm")
                    s.wait("vector", "wg")
                    s.wait("vector", "xw")
                    s.op("vector", "tensor_add", out=big3[0:128, 0:1024],
                         in0=psC[:, 0:1024], in1=bvrep_s[:, 0:1024], incs=[("vv", 1)])
                    s.op("vector", "tensor_add", out=big3[:, 1024:1040],
                         in0=psD[:, 0:16], in1=bvrep_s[:, 1024:1040], incs=[("dv", 1)])
                    s.wait("gpsimd", "dv")
                    s.op("gpsimd", "dma_start",
                         out=lc_v.rearrange("(h t) f -> t h f", h=NCH)[m0:m0 + 128, :, :],
                         in_=big3[:, 0:1040].rearrange("p (h f) -> p h f", f=130),
                         incs=[("xw", 16)])
                s.wait("gpsimd", "ev")
                s.op("gpsimd", "dma_start",
                     out=sp_q.rearrange("(c p) t -> p c t", p=128)[:, :, t0:t0 + TT],
                     in_=qs[:, :, :], incs=[("sp", 16)])
                s.op("gpsimd", "dma_start",
                     out=lc_k.rearrange("(c p) t -> p c t", p=128)[:, :, t0:t0 + TT],
                     in_=ks[:, :, :], incs=[("xw", 16)])
            s.wait("gpsimd", "xw")

            def pub_kv(e, sems, layer=layer):
                e.dma_start(out=sh_k[layer][DynSlice(get_par(e, "parD"), D), :],
                            in_=lc_k).then_inc(sems["xw"], 16)
                e.dma_start(out=sh_v[layer][DynSlice(get_par(e, "parV"), NCH * TC), :],
                            in_=lc_v).then_inc(sems["xw"], 16)
            s.raw("gpsimd", pub_kv)
            s.cnt["xw"] += 32
            s.wait("gpsimd", "xw")
            emit_flag(2 * layer + 1)
            emit_poll(2 * layer + 1)
            s.barrier()

        def attention(layer):
            wao = big3[:, :].rearrange("p (c o) -> p c o", o=D)
            s.op("gpsimd", "dma_start", out=wao[:, :, :],
                 in_=w_ao[layer].rearrange("(c p) o -> p c o", p=128), incs=[("wg", 16)])
            NKC = S // 128
            hseq = 0
            qtg = [0]
            dv_cp = {}
            dv_mul = {}
            mm_after_hp = {}
            for qh in range(2):
                q0 = qh * 2048
                for hg in range(2):
                    s.wait("sync", "mm")
                    s.op("sync", "dma_start",
                         out=big2[:, 8192:16384].rearrange("p (i t) -> p i t", t=2048),
                         in_=sp_q.rearrange("(h p) t -> p h t", p=128)
                         [:, hg * 4:(hg + 1) * 4, q0:q0 + 2048],
                         incs=[("ld", 16)])
                    qg = big2[:, 8192:16384].rearrange("p (i t) -> p i t", t=2048)
                    s.wait("vector", "sp")
                    for hpi in range(4):
                        hp = hg * 4 + hpi
                        sl = hseq % 2
                        base = sl * 16512
                        kb = big1[:, base:base + 8192]
                        vb = big1[:, base + 8192:base + 16512].rearrange(
                            "p (r f) -> p r f", f=130)
                        if hseq >= 2:
                            s.wait("sync", "mm", mm_after_hp[hseq - 2])
                        for o in range(2):
                            s.op("sync", "dma_start",
                                 out=kb[:, o * 4096:(o + 1) * 4096],
                                 in_=sh_k[layer][o * D + hp * 128:o * D + (hp + 1) * 128, :],
                                 incs=[("ld", 16)])
                            s.op("sync", "dma_start",
                                 out=vb[:, o * 32:(o + 1) * 32, :],
                                 in_=sh_v[layer][(o * NCH + hp) * TC:(o * NCH + hp + 1) * TC, :]
                                 .rearrange("(r p) f -> p r f", p=128),
                                 incs=[("ld", 16)])
                        s.wait("tensor", "ld")
                        for qt in range(4):
                            qq = qt * TT
                            ah = (qt % 2) * TT
                            qv = qtg[0]
                            if qv >= 2:
                                s.wait("tensor", "dv", dv_mul[qv - 2])
                            if qv >= 1:
                                s.wait("tensor", "dv", dv_cp[qv - 1])

                            def emit_pv(kc):
                                pt = ptx[:, kc % 2, :]
                                s.wait("tensor", "ev", ev_exp[kc])
                                s.op("tensor", "matmul", psC[0:65, ah:ah + TT],
                                     vb[:, kc, 0:65], pt[:, 0:TT],
                                     start=(kc == 0), stop=(kc == NKC - 1))
                                s.op("tensor", "matmul", psD[0:65, ah:ah + TT],
                                     vb[:, kc, 65:130], pt[:, TT:2 * TT],
                                     start=(kc == 0), stop=(kc == NKC - 1),
                                     incs=[("mm", 1)])
                            ev_exp = {}
                            for kc in range(NKC):
                                ph = psA if kc % 2 == 0 else psB
                                if kc >= 2:
                                    s.wait("tensor", "ev", ev_exp[kc - 2])
                                s.op("tensor", "matmul", ph[:, 0:TT],
                                     kb[0:64, kc * 128:(kc + 1) * 128],
                                     qg[0:64, hpi, qq:qq + TT],
                                     start=True, stop=True)
                                s.op("tensor", "matmul", ph[:, TT:2 * TT],
                                     kb[64:128, kc * 128:(kc + 1) * 128],
                                     qg[64:128, hpi, qq:qq + TT],
                                     start=True, stop=True, incs=[("mm", 1)])
                                s.wait("scalar", "mm")
                                pt = ptx[:, kc % 2, :]
                                s.op("scalar", "activation", out=pt, in_=ph[:, :],
                                     func=AF.Exp, scale=cstc(CST_SCALE),
                                     bias=cstc(CST_NEGC), incs=[("ev", 1)])
                                ev_exp[kc] = s.cnt["ev"]
                                if kc >= 1:
                                    emit_pv(kc - 1)
                            emit_pv(NKC - 1)
                            s.wait("vector", "mm")
                            s.op("vector", "reciprocal", out=rcpb[0:1, 0:TT],
                                 in_=psC[64:65, ah:ah + TT])
                            s.op("vector", "reciprocal", out=rcpb[0:1, TT:2 * TT],
                                 in_=psD[64:65, ah:ah + TT], incs=[("dv", 1)])
                            s.wait("tensor", "dv")
                            s.op("tensor", "matmul", psA[:, 0:TT], onesb[0:1, :],
                                 rcpb[0:1, 0:TT], start=True, stop=True)
                            s.op("tensor", "matmul", psA[:, TT:2 * TT], onesb[0:1, :],
                                 rcpb[0:1, TT:2 * TT], start=True, stop=True,
                                 incs=[("mm", 1)])
                            s.wait("vector", "mm")
                            s.op("vector", "tensor_copy", out=f32p[:, 0:1024],
                                 in_=psA[:, :], incs=[("dv", 1)])
                            dv_cp[qtg[0]] = s.cnt["dv"]
                            s.wait("vector", "dv")
                            og = big2[:, 0:8192].rearrange("p (i t) -> p i t", t=2048)
                            s.op("vector", "tensor_mul",
                                 out=og[0:64, hpi, qq:qq + TT],
                                 in0=psC[0:64, ah:ah + TT], in1=f32p[0:64, 0:TT],
                                 incs=[("vv", 1)])
                            s.op("vector", "tensor_mul",
                                 out=og[64:128, hpi, qq:qq + TT],
                                 in0=psD[0:64, ah:ah + TT], in1=f32p[64:128, TT:2 * TT],
                                 incs=[("dv", 1)])
                            dv_mul[qtg[0]] = s.cnt["dv"]
                            qtg[0] += 1
                        mm_after_hp[hseq] = s.cnt["mm"]
                        hseq += 1
                    s.wait("gpsimd", "dv")
                    s.op("gpsimd", "dma_start",
                         out=sp_o.rearrange("p (h t) -> p h t", t=TC)
                         [:, hg * 4:(hg + 1) * 4, q0:q0 + 2048],
                         in_=big2[:, 0:8192].rearrange("p (i t) -> p i t", t=2048),
                         incs=[("sp", 16)])
            s.barrier()
            onv = big2[:, 0:4096].rearrange("p (h t) -> p h t", t=TT)
            for tt in range(NTT):
                t0 = tt * TT
                s.wait("sync", "mm")
                s.wait("sync", "sp")
                s.op("sync", "dma_start", out=onv[:, :, :],
                     in_=sp_o.rearrange("p (h t) -> p h t", t=TC)[:, :, t0:t0 + TT],
                     incs=[("ld", 16)])
                s.wait("tensor", "ld")
                for dc in range(NCH):
                    ph = psC[:, (dc % 2) * TT:(dc % 2 + 1) * TT]
                    if tt or dc >= 2:
                        s.wait("tensor", "dv")
                    for hp in range(NCH):
                        s.op("tensor", "matmul", ph,
                             wao[:, hp, dc * 128:(dc + 1) * 128],
                             onv[:, hp, :], start=(hp == 0), stop=(hp == NCH - 1),
                             incs=([("mm", 1)] if hp == NCH - 1 else []))
                    s.wait("vector", "mm")
                    s.op("vector", "tensor_scalar_add", out=f32p[:, 0:TT], in0=ph,
                         scalar1=bc(layer, "b_ao", dc), incs=[("vv", 1)])
                    s.wait("vector", "vv")
                    s.op("vector", "tensor_add",
                         out=big0[:, dc, t0:t0 + TT],
                         in0=f32p[:, 0:TT],
                         in1=big0[:, dc, t0:t0 + TT], incs=[("dv", 1)])
            s.barrier()

        def ffn(layer):
            s.op("gpsimd", "dma_start", out=sp_x,
                 in_=big0[:, :, :].rearrange("p c t -> p (c t)"), incs=[("sp", 16)])
            s.wait("gpsimd", "sp")
            wf1 = big0[:, :, :]
            wf2 = big1[:, 0:32768].rearrange("p (c o) -> p c o", o=D)
            s.op("gpsimd", "dma_start", out=big0[:, :, :],
                 in_=w_f1[layer].rearrange("(c p) o -> p c o", p=128), incs=[("wg", 16)])
            s.op("gpsimd", "dma_start", out=wf2[:, :, :],
                 in_=w_f2[layer].rearrange("(c p) o -> p c o", p=128), incs=[("wg", 16)])
            s.wait("tensor", "wg")
            xt = big3[:, 0:4096].rearrange("p (c t) -> p c t", t=TT)
            v3 = big3[:, 4096:8192].rearrange("p (c t) -> p c t", t=TT)
            hb = big2[:, :].rearrange("p (c t) -> p c t", t=TT)
            for tt in range(NTT):
                t0 = tt * TT
                s.wait("gpsimd", "mm")
                s.op("gpsimd", "dma_start", out=big3[:, 0:4096],
                     in_=sp_x.rearrange("p (c t) -> p c t", t=TC)[:, :, t0:t0 + TT],
                     incs=[("wg", 16)])
                s.wait("tensor", "wg")
                for fc in range(32):
                    ph = psA[:, (fc % 2) * TT:(fc % 2 + 1) * TT]
                    if tt or fc >= 2:
                        s.wait("tensor", "ev")
                    for c in range(NCH):
                        s.op("tensor", "matmul", ph, wf1[:, c, fc * 128:(fc + 1) * 128],
                             xt[:, c, :], start=(c == 0), stop=(c == NCH - 1),
                             incs=([("mm", 1)] if c == NCH - 1 else []))
                    s.wait("scalar", "mm")
                    s.op("scalar", "activation", out=hb[:, fc, :], in_=ph,
                         func=AF.Gelu, bias=bc(layer, "b_f1", fc), incs=[("ev", 1)])
                s.wait("tensor", "ev")
                s.wait("tensor", "dv")
                PS = [psA, psB, psC, psD]
                for fc in range(32):
                    for dc in range(NCH):
                        ph = PS[dc // 2][:, (dc % 2) * TT:(dc % 2 + 1) * TT]
                        s.op("tensor", "matmul", ph, wf2[:, fc, dc * 128:(dc + 1) * 128],
                             hb[:, fc, :], start=(fc == 0), stop=(fc == 31),
                             incs=([("mm", 1)] if (fc == 31 and dc == NCH - 1) else []))
                s.wait("vector", "mm")
                for dc in range(NCH):
                    ph = PS[dc // 2][:, (dc % 2) * TT:(dc % 2 + 1) * TT]
                    s.op("vector", "tensor_scalar_add", out=f32p[:, 0:TT], in0=ph,
                         scalar1=bc(layer, "b_f2", dc), incs=[("vv", 1)])
                    s.wait("vector", "vv")
                    s.op("vector", "tensor_add", out=v3[:, dc, :], in0=f32p[:, 0:TT],
                         in1=xt[:, dc, :], incs=[("dv", 1)])
                s.wait("gpsimd", "dv")
                s.op("gpsimd", "dma_start",
                     out=sp_x2.rearrange("p (c t) -> p c t", t=TC)[:, :, t0:t0 + TT],
                     in_=big3[:, 4096:8192].rearrange("p (c t) -> p c t", t=TT),
                     incs=[("sp", 16)])
            s.wait("gpsimd", "sp")
            s.op("gpsimd", "dma_start", out=big0[:, :, :].rearrange("p c t -> p (c t)"),
                 in_=sp_x2, incs=[("wg", 16)])
            s.barrier()

        def outer_end():
            for tt in range(NTT):
                t0 = tt * TT
                s.wait("gpsimd", "dv")
                s.op("gpsimd", "dma_start", out=big2[:, 0:4096],
                     in_=xT.rearrange("(c p) t -> p c t", p=128)[:, :, t0:t0 + TT],
                     incs=[("wg", 16)])
                s.wait("vector", "wg")
                x0t = big2[:, 0:4096].rearrange("p (c t) -> p c t", t=TT)
                for c in range(NCH):
                    s.op("vector", "tensor_add", out=big0[:, c, t0:t0 + TT],
                         in0=big0[:, c, t0:t0 + TT], in1=x0t[:, c, :],
                         incs=[("dv", 1)])
            s.barrier()
            layernorm(0, None, router=True, spill_dst=sp_z2)
            emit_masks()
            s.barrier()

        def emit_masks():
            for tg in range(32):
                if tg > 0:
                    s.wait("vector", "mm")
                s.op("vector", "reduce_max", out=mge(0, 1),
                     in_=logit[:, tg, :], axis=mybir.AxisListType.X, incs=[("vv", 1)])
                s.wait("vector", "vv")
                s.op("vector", "tensor_scalar", out=mge(1, NEXP),
                     in0=logit[:, tg, :], scalar1=mge(0, 1), scalar2=None,
                     op0=ALU.is_ge, incs=[("vv", 1)])
                s.wait("vector", "vv")
                s.op("vector", "tensor_tensor_scan", out=mge(2, NEXP),
                     data0=ones8[:, 0:NEXP], data1=mge(1, NEXP), initial=0.0,
                     op0=ALU.mult, op1=ALU.add, incs=[("vv", 1)])
                s.wait("vector", "vv")
                s.op("vector", "tensor_scalar", out=mge(2, NEXP),
                     in0=mge(2, NEXP), scalar1=1.0, scalar2=None,
                     op0=ALU.is_equal, incs=[("vv", 1)])
                s.wait("vector", "vv")
                s.op("vector", "tensor_mul", out=mge(1, NEXP),
                     in0=mge(1, NEXP), in1=mge(2, NEXP),
                     incs=[("dv", 1)])
                s.wait("tensor", "dv")
                s.wait("tensor", "dv")
                s.op("tensor", "transpose", psD[0:NEXP, 0:128],
                     mge(1, NEXP), ident_s[:, :], incs=[("mm", 1)])
                s.wait("vector", "mm")
                s.op("vector", "tensor_copy",
                     out=big3[0:NEXP, 4096 + tg * 128:4096 + (tg + 1) * 128],
                     in_=psD[0:NEXP, 0:128], incs=[("dv", 1)])

        def out_init():
            slot_ost = {0: 0, 1: 0}
            for tt in range(NTT):
                t0 = tt * TT
                for dc in range(NCH):
                    ph = psA[:, (dc % 2) * TT:(dc % 2 + 1) * TT]
                    if tt or dc >= 2:
                        s.wait("tensor", "dv")
                    s.op("tensor", "matmul", ph, b2r_s[:, dc * 128:(dc + 1) * 128],
                         big3[0:NEXP, 4096 + t0:4096 + t0 + TT], start=True, stop=True,
                         incs=[("mm", 1)])
                    s.wait("vector", "mm")
                    s.wait("vector", "ost", slot_ost[dc % 2])
                    s.op("vector", "tensor_add", out=stg(dc % 2),
                         in0=big0[:, dc, t0:t0 + TT], in1=ph, incs=[("dv", 1)])
                    s.wait("gpsimd", "dv")
                    s.op("gpsimd", "dma_start",
                         out=out.rearrange("(c p) t -> p c t", p=128)[:, dc, t0:t0 + TT],
                         in_=stg(dc % 2), incs=[("ost", 16)])
                    slot_ost[dc % 2] = s.cnt["ost"]
            s.barrier()

        def moe():
            slot_ost = {0: 0, 1: 0}
            w1b = big0[:, :, :]
            w2b = big1[:, 0:32768].rearrange("p (c o) -> p c o", o=D)
            zt = big3[:, 0:4096].rearrange("p (c t) -> p c t", t=TT)
            mb = f32p[:, 3 * TT:4 * TT]
            hb = big2[:, :].rearrange("p (c t) -> p c t", t=TT)
            for e in range(NEXP):
                s.wait("gpsimd", "mm")
                s.op("gpsimd", "dma_start", out=w1b[:, :, :],
                     in_=w1[e].rearrange("(c p) o -> p c o", p=128), incs=[("wg", 16)])
                s.op("gpsimd", "dma_start", out=w2b[:, :, :],
                     in_=w2[e].rearrange("(c p) o -> p c o", p=128), incs=[("wg", 16)])
                s.op("sync", "dma_start", out=b1c_s[:, :],
                     in_=b1c[:, e * 32:(e + 1) * 32], incs=[("ld", 16)])
                s.wait("scalar", "ld")
                ost_before = s.cnt["ost"]
                s.wait("tensor", "wg")
                for tt in range(NTT):
                    t0 = tt * TT
                    s.wait("sync", "mm")
                    s.op("sync", "dma_start", out=big3[:, 0:4096],
                         in_=sp_z2.rearrange("p (c t) -> p c t", t=TC)[:, :, t0:t0 + TT],
                         incs=[("ld", 16)])
                    s.wait("tensor", "dv")
                    s.op("tensor", "matmul", psD[:, 512:1024],
                         basis_s[:, e * 128:(e + 1) * 128],
                         big3[0:NEXP, 4096 + t0:4096 + t0 + TT],
                         start=True, stop=True, incs=[("mm", 1)])
                    s.wait("vector", "mm")
                    s.op("vector", "tensor_copy", out=mb, in_=psD[:, 512:1024],
                         incs=[("dv", 1)])
                    s.wait("tensor", "wg")
                    s.wait("tensor", "ld")
                    s.wait("tensor", "dv")
                    for fc in range(32):
                        ph = psA[:, (fc % 2) * TT:(fc % 2 + 1) * TT]
                        if tt or fc >= 2 or e:
                            s.wait("tensor", "ev")
                        for c in range(NCH):
                            s.op("tensor", "matmul", ph, w1b[:, c, fc * 128:(fc + 1) * 128],
                                 zt[:, c, :], start=(c == 0), stop=(c == NCH - 1),
                                 incs=([("mm", 1)] if c == NCH - 1 else []))
                        s.wait("scalar", "mm")
                        s.op("scalar", "activation", out=hb[:, fc, :], in_=ph,
                             func=AF.Relu, bias=b1c_s[:, fc:fc + 1],
                             incs=[("ev", 1)])
                    s.wait("tensor", "ev")
                    PS = [psA, psB, psC, psD]
                    for fc in range(32):
                        for dc in range(NCH):
                            ph = PS[dc // 2][:, (dc % 2) * TT:(dc % 2 + 1) * TT]
                            s.op("tensor", "matmul", ph, w2b[:, fc, dc * 128:(dc + 1) * 128],
                                 hb[:, fc, :], start=(fc == 0), stop=(fc == 31),
                                 incs=([("mm", 1)] if (fc == 31 and dc == NCH - 1) else []))
                    s.wait("vector", "mm")
                    for dc in range(NCH):
                        ph = PS[dc // 2][:, (dc % 2) * TT:(dc % 2 + 1) * TT]
                        s.wait("vector", "ost", slot_ost[dc % 2])
                        s.op("vector", "tensor_mul", out=stg(dc % 2), in0=ph,
                             in1=mb, incs=[("dv", 1)])
                        s.wait("gpsimd", "dv")
                        s.wait("gpsimd", "ost", ost_before)
                        s.op("gpsimd", "dma_start",
                             out=out.rearrange("(c p) t -> p c t", p=128)[:, dc, t0:t0 + TT],
                             in_=stg(dc % 2), accum_op=ALU.add, incs=[("ost", 16)])
                        slot_ost[dc % 2] = s.cnt["ost"]
            s.wait("gpsimd", "ost")
            s.barrier()

        def dump_stream():
            s.barrier()
            s.op("gpsimd", "dma_start",
                 out=out.rearrange("(c p) t -> p c t", p=128),
                 in_=big0[:, :, :], incs=[("ost", 16)])
            s.wait("gpsimd", "ost")
            s.barrier()

        phase_list = [("preload", preload),
                      ("ln0", lambda: layernorm(0, None)),
                      ("m1t0", lambda: mamba_p1(0, part="tiles")),
                      ("m1x0", lambda: mamba_p1(0)),
                      ("m2b0", lambda: (mamba_p1(0), mamba_p2(0, part="b")))]
        for l in range(L):
            phase_list += [
                (f"mamba{l}", lambda l=l: (mamba_p1(l), mamba_p2(l))),
                (f"ln1_{l}", lambda l=l: layernorm(l, ("g1", "be1"))),
                (f"qkv{l}", lambda l=l: qkv(l)),
                (f"attn{l}", lambda l=l: attention(l)),
                (f"ln2_{l}", lambda l=l: layernorm(l, ("g2", "be2"))),
                (f"ffn{l}", lambda l=l: ffn(l)),
            ]
        phase_list += [("outer", outer_end), ("outinit", out_init), ("moe", moe)]
        done = False
        for nm, fn in phase_list:
            SUB = ("m1t0", "m1x0", "m2b0")
            if upto not in SUB and nm in SUB:
                continue
            if upto in SUB:
                if nm in SUB and nm != upto:
                    continue
                if nm == "mamba0":
                    continue
            fn()
            if upto is not None and nm == upto:
                dump_stream()
                done = True
                break
        assert upto is None or done, f"unknown phase {upto}"

        lp = nc.allow_low_precision(reason="bf16 megakernel per problem tolerance")
        lp.__enter__()

        @block.sync
        def _(sync):
            for fn in s.prog["sync"]:
                fn(sync, sems)

        @block.gpsimd
        def _(gpsimd):
            for fn in s.prog["gpsimd"]:
                fn(gpsimd, sems)

        @block.tensor
        def _(tensor):
            for fn in s.prog["tensor"]:
                fn(tensor, sems)

        @block.scalar
        def _(scalar):
            for fn in s.prog["scalar"]:
                fn(scalar, sems)

        @block.vector
        def _(vector):
            for fn in s.prog["vector"]:
                fn(vector, sems)

        lp.__exit__(None, None, None)

    return nc



def _prep_maps(inputs, nonce):
    f32 = np.float32

    def T(a):
        return np.ascontiguousarray(np.asarray(a, f32).T)

    gi = {k: np.asarray(v, f32) for k, v in inputs.items()}
    x = gi["L0_attn_x"]

    w_in = np.stack([T(gi["L0_attn_W_in"][l]) for l in range(L)])
    w_dt = np.stack([T(gi["L0_attn_W_dt"][l]) for l in range(L)])
    w_xp = np.stack([T(gi["L0_attn_W_xp"][l]) for l in range(L)])
    w_out = np.stack([T(gi["L0_attn_W_out"][l]) for l in range(L)])
    wqkv = gi["L0_attn_W_qkv"]
    w_q = np.stack([T(wqkv[l, 0:D]) for l in range(L)])
    w_k = np.stack([T(wqkv[l, D:2 * D]) for l in range(L)])
    w_v = np.zeros((L, D, H * 65), f32)
    bvrep = np.zeros((L, 128, H * 65), f32)
    bqkv = gi["L0_attn_b_qkv"]
    for l in range(L):
        vT = T(wqkv[l, 2 * D:3 * D])
        for h in range(H):
            w_v[l, :, h * 65:h * 65 + 64] = vT[:, h * 64:(h + 1) * 64]
            bvrep[l, :, h * 65:h * 65 + 64] = bqkv[l, 2 * D + h * 64:2 * D + (h + 1) * 64][None, :]
            bvrep[l, :, h * 65 + 64] = 1.0
    w_ao = np.stack([T(gi["L0_attn_W_ao"][l]) for l in range(L)])
    w_f1 = np.stack([T(gi["L0_attn_W_f1"][l]) for l in range(L)])
    w_f2 = np.stack([T(gi["L0_attn_W_f2"][l]) for l in range(L)])

    def cols(v, n):
        return np.ascontiguousarray(np.asarray(v, f32).reshape(n, 128).T)

    bcols = np.zeros((128, L * 160), f32)
    for l in range(L):
        b0 = l * 160
        bcols[:, b0 + 0:b0 + 32] = cols(gi["L0_attn_b_in"][l], 32)
        bcols[:, b0 + 32:b0 + 40] = cols(gi["L0_attn_b_dt"][l], 8)
        bcols[:, b0 + 40:b0 + 48] = cols(gi["L0_attn_b_out"][l], 8)
        bcols[:, b0 + 48:b0 + 56] = cols(bqkv[l, 0:D], 8)
        bcols[:, b0 + 56:b0 + 64] = cols(bqkv[l, D:2 * D], 8)
        bcols[:, b0 + 64:b0 + 72] = cols(gi["L0_attn_b_ao"][l], 8)
        bcols[:, b0 + 72:b0 + 80] = cols(gi["L0_attn_g1"][l], 8)
        bcols[:, b0 + 80:b0 + 88] = cols(gi["L0_attn_be1"][l], 8)
        bcols[:, b0 + 88:b0 + 96] = cols(gi["L0_attn_g2"][l], 8)
        bcols[:, b0 + 96:b0 + 104] = cols(gi["L0_attn_be2"][l], 8)
        bcols[:, b0 + 104:b0 + 136] = cols(gi["L0_attn_b_f1"][l], 32)
        bcols[:, b0 + 136:b0 + 144] = cols(gi["L0_attn_b_f2"][l], 8)
        bcols[:, b0 + 144:b0 + 152] = cols(gi["L0_attn_Dp"][l], 8)
    bxp = np.ascontiguousarray(gi["L0_attn_b_xp"].T)
    alog = np.ascontiguousarray(gi["L0_attn_A_log"].T)
    w_r = T(gi["L0_ffn_Wr"])
    br_rep = np.broadcast_to(gi["L0_ffn_br"][None, :], (128, NEXP)).copy()
    w1 = np.stack([T(gi["L0_ffn_W1"][e]) for e in range(NEXP)])
    b1c = np.concatenate([cols(gi["L0_ffn_b1"][e], 32) for e in range(NEXP)], axis=1)
    w2 = np.stack([T(gi["L0_ffn_W2"][e]) for e in range(NEXP)])
    b2r = np.asarray(gi["L0_ffn_b2"], f32)

    ones_bf = np.ones((128, 128), ml_dtypes.bfloat16)
    ones8f = np.ones((128, NEXP), f32)
    identf = np.eye(128, dtype=f32)
    basis = np.zeros((NEXP, NEXP * 128), ml_dtypes.bfloat16)
    for e in range(NEXP):
        basis[e, e * 128:(e + 1) * 128] = 1.0
    cst = np.zeros((128, 8), f32)
    cst[:, 0] = -CSHIFT
    cst[:, 1] = SCALE
    cst[:, 2] = 1.0 / D
    cst[:, 3] = EPS
    cst[:, 4] = -0.5
    cst[:, 5] = 1.0
    flagrows = np.zeros((1, NEP), np.int32)
    for ep in range(NEP):
        flagrows[0, ep] = nonce + ep

    shared = dict(w_in=w_in, w_dt=w_dt, w_xp=w_xp, w_out=w_out, w_q=w_q, w_k=w_k,
                  w_v=w_v, w_ao=w_ao, w_f1=w_f1, w_f2=w_f2, bcols=bcols, bxp=bxp,
                  alog=alog, bvrep=bvrep, w_r=w_r, br_rep=br_rep,
                  w1=w1, b1c=b1c, w2=w2, b2r=b2r, ones_bf=ones_bf, ones8f=ones8f,
                  identf=identf, basis=basis, cst=cst, flagrows=flagrows)

    maps = []
    for c in range(8):
        cc = c % 4
        b, par = cc // 2, cc % 2
        xs = np.ascontiguousarray(x[b, par * TC:(par + 1) * TC, :].T)
        mS = np.zeros((64, 1), f32)
        if par == 1:
            mS[:, 0] = 1.0
        m = dict(shared)
        m["xT"] = xs
        m["maskS"] = mS
        maps.append(m)
    return maps


_CACHE = {}


def kernel(**inputs):
    if "nc" not in _CACHE:
        _CACHE["nc"] = build_program()
    nc = _CACHE["nc"]
    nonce = int(time.time() * 1000) % 1000000 * 100 + 13
    maps = _prep_maps(inputs, nonce)
    res = run_bass_kernel_spmd(nc, maps, core_ids=list(range(8)))
    halves = []
    for cc in range(4):
        o = np.asarray(res.results[cc]["out"])
        halves.append(o.T)
    outb0 = np.concatenate([halves[0], halves[1]], axis=0)
    outb1 = np.concatenate([halves[2], halves[3]], axis=0)
    return np.stack([outb0, outb1], axis=0).astype(np.float32)


# version 12
# speedup vs baseline: 1.0027x; 1.0027x over previous
import numpy as np
import ml_dtypes
import time
from contextlib import ExitStack

import concourse.bass as bass
import concourse.mybir as mybir
from concourse.bass_types import DynSlice
from concourse.bass_utils import run_bass_kernel_spmd

F32 = mybir.dt.float32
BF16 = mybir.dt.bfloat16
I32 = mybir.dt.int32
AF = mybir.ActivationFunctionType
ALU = mybir.AluOpType

B, S, D = 2, 8192, 1024
L = 4
DI = 2048
NST = 64
H, DH = 16, 64
DF = 4096
NEXP = 8
TC = 4096
TT = 512
NTT = TC // TT
NCH = D // 128
EPS = 1e-5
SCALE = 1.0 / 8.0
CSHIFT = 20.0
NEP = 2 * L

ENGS = ("sync", "gpsimd", "tensor", "scalar", "vector")


class Sched:
    def __init__(self):
        self.prog = {e: [] for e in ENGS}
        self.cnt = {}

    def sem(self, name):
        self.cnt.setdefault(name, 0)
        return name

    def raw(self, eng, fn):
        self.prog[eng].append(fn)

    def op(self, eng, attr, *a, incs=(), **k):
        for sname, v in incs:
            self.sem(sname)
            self.cnt[sname] += v

        def fn(e, sems, attr=attr, a=a, k=k, incs=incs):
            inst = getattr(e, attr)(*a, **k)
            for sname, v in incs:
                inst.then_inc(sems[sname], v)
        self.prog[eng].append(fn)

    def wait(self, eng, sname, val=None):
        self.sem(sname)
        v = self.cnt[sname] if val is None else val

        def fn(e, sems, sname=sname, v=v):
            e.wait_ge(sems[sname], v)
        self.prog[eng].append(fn)
        return v

    def barrier(self, engs=ENGS):
        for e in engs:
            self.sem("ph_" + e)
        for e in engs:
            self.op(e, "nop", incs=[("ph_" + e, 1)])
        targets = {e: self.cnt["ph_" + e] for e in engs}
        for e in engs:
            for o in engs:
                if o != e:
                    self.wait(e, "ph_" + o, targets[o])


def build_program(upto=None):
    nc = bass.Bass("TRN2", target_bir_lowering=False, debug=False)
    s = Sched()

    def din(name, shape, dtype=F32):
        return nc.dram_tensor(name, shape, dtype, kind="ExternalInput").ap()

    xT = din("xT", [D, TC])
    w_in = din("w_in", [L, D, 2 * DI])
    w_dt = din("w_dt", [L, DI, D])
    w_xp = din("w_xp", [L, DI, NST])
    w_out = din("w_out", [L, DI, D])
    w_q = din("w_q", [L, D, D])
    w_k = din("w_k", [L, D, D])
    w_v = din("w_v", [L, D, H * 65])
    w_ao = din("w_ao", [L, D, D])
    w_f1 = din("w_f1", [L, D, DF])
    w_f2 = din("w_f2", [L, DF, D])
    bcols = din("bcols", [128, L * 160])
    bxp = din("bxp", [64, L])
    alog = din("alog", [64, L])
    bvrep = din("bvrep", [L, 128, H * 65])
    maskS = din("maskS", [64, 1])
    w_r = din("w_r", [D, NEXP])
    br_rep = din("br_rep", [128, NEXP])
    w1 = din("w1", [NEXP, D, DF])
    b1c = din("b1c", [128, NEXP * 32])
    w2 = din("w2", [NEXP, DF, D])
    b2r = din("b2r", [NEXP, D])
    ones_bf = din("ones_bf", [128, 128], BF16)
    ones8f = din("ones8f", [128, NEXP])
    basis = din("basis", [NEXP, NEXP * 128], BF16)
    identf = din("identf", [128, 128])
    cst = din("cst", [128, 8])
    flagrows = din("flagrows", [1, NEP], I32)

    out = nc.dram_tensor("out", [D, TC], F32, kind="ExternalOutput").ap()

    sp_xi = nc.dram_tensor("sp_xi", [128, 16 * TC], BF16).ap()
    sp_q = nc.dram_tensor("sp_q", [D, TC], BF16).ap()
    sp_o = nc.dram_tensor("sp_o", [128, NCH * TC], BF16).ap()
    lc_k = nc.dram_tensor("lc_k", [D, TC], BF16).ap()
    lc_v = nc.dram_tensor("lc_v", [NCH * TC, 130], BF16).ap()
    sp_x = nc.dram_tensor("sp_x", [128, NCH * TC], BF16).ap()
    sp_x2 = nc.dram_tensor("sp_x2", [128, NCH * TC], BF16).ap()
    sp_z2 = nc.dram_tensor("sp_z2", [128, NCH * TC], BF16).ap()

    sh_k = [nc.dram_tensor(f"sh_k{l}", [2 * D, TC], BF16, addr_space="Shared").ap()
            for l in range(L)]
    sh_v = [nc.dram_tensor(f"sh_v{l}", [2 * NCH * TC, 130], BF16, addr_space="Shared").ap()
            for l in range(L)]
    sh_sc = [nc.dram_tensor(f"sh_sc{l}", [2 * 64, 2], F32, addr_space="Shared").ap()
             for l in range(L)]
    sh_fl = nc.dram_tensor("sh_fl", [NEP, 2], I32, addr_space="Shared").ap()

    with ExitStack() as ctx:
        def sb(name, shape, dt):
            return ctx.enter_context(nc.sbuf_tensor(name, shape, dt))

        big0 = sb("big0", [128, NCH, TC], BF16)
        big1 = sb("big1", [128, 33024], BF16)
        big2 = sb("big2", [128, 16384], BF16)
        big3 = sb("big3", [128, 8192], BF16)
        f32p = sb("f32p", [128, 2048], F32)
        ptx = sb("ptx", [128, 2, 1024], BF16)
        rcpb = sb("rcpb", [1, 1024], BF16)
        sc_a = sb("sc_a", [64, TT], F32)
        sc_t = sb("sc_t", [64, TT], F32)
        sc_u = sb("sc_u", [64, TT], F32)
        sc_z = sb("sc_z", [64, TT], F32)
        sc_st = sb("sc_st", [128, 4], F32)
        scmr = sb("scmr", [128, 128], BF16)
        msk_s = sb("msk_s", [128, 1], F32)
        onesb = sb("onesb", [128, 128], BF16)
        ones8 = sb("ones8", [128, NEXP], F32)
        basis_s = sb("basis_s", [NEXP, NEXP * 128], BF16)
        ident_s = sb("ident_s", [128, 128], F32)
        cst_s = sb("cst_s", [128, 8], F32)
        bcols_s = sb("bcols_s", [128, 160], F32)
        bxp_s = sb("bxp_s", [64, L], F32)
        a_d = sb("a_d", [64, L], F32)
        bvrep_s = sb("bvrep_s", [128, H * 65], BF16)
        b1c_s = sb("b1c_s", [128, 32], F32)
        b2r_s = sb("b2r_s", [NEXP, D], BF16)
        brr_s = sb("brr_s", [128, NEXP], F32)
        wr_s = sb("wr_s", [128, NCH, NEXP], F32)
        fl_s = sb("fl_s", [1, NEP], I32)
        pf_s = sb("pf_s", [1, 2], I32)
        logit = sb("logit", [128, 32, NEXP], F32)

        def stg(sl):
            return f32p[:, sl * TT:(sl + 1) * TT]

        def mge(a, n):
            return f32p[:, 1024 + a * NEXP:1024 + a * NEXP + n]
        psA = ctx.enter_context(nc.psum_tensor("psA", [128, 1024], F32))
        psB = ctx.enter_context(nc.psum_tensor("psB", [128, 1024], F32))
        psC = ctx.enter_context(nc.psum_tensor("psC", [128, 1024], F32))
        psD = ctx.enter_context(nc.psum_tensor("psD", [128, 1024], F32))

        sems = {}
        for nm in (["ld", "wg", "mm", "ev", "dv", "sp", "xw", "pp", "ost",
                    "vv", "sv"]
                   + ["ph_" + e for e in ENGS]):
            sems[nm] = ctx.enter_context(nc.semaphore(nm))
            s.sem(nm)

        block = ctx.enter_context(nc.Block())

        par_cache = {}

        def get_par(e, key="par"):
            ck = (id(e), key)
            if ck not in par_cache:
                base = (id(e), "par")
                if base not in par_cache:
                    par_cache[base] = e.partition_id() % 2
                par = par_cache[base]
                if key == "parD":
                    par_cache[ck] = par * D
                elif key == "parTC":
                    par_cache[ck] = par * TC
                elif key == "par64":
                    par_cache[ck] = par * 64
                elif key == "parV":
                    par_cache[ck] = par * (NCH * TC)
            return par_cache[ck]

        poll_regs = {}

        def get_poll_regs(e):
            if not poll_regs:
                poll_regs["c"] = e.alloc_register("pollc")
                poll_regs["w"] = e.alloc_register("pollw")
                poll_regs["v0"] = e.alloc_register("pollv0")
                poll_regs["v1"] = e.alloc_register("pollv1")
                poll_regs["vm"] = e.alloc_register("pollvm")
                poll_regs["t0"] = e.alloc_register("pollt0")
                poll_regs["t1"] = e.alloc_register("pollt1")
                e.reg_mov(poll_regs["w"], 0)
            return poll_regs

        BC = {"b_in": (0, 32), "b_dt": (32, 8), "b_out": (40, 8), "b_q": (48, 8),
              "b_k": (56, 8), "b_ao": (64, 8), "g1": (72, 8), "be1": (80, 8),
              "g2": (88, 8), "be2": (96, 8), "b_f1": (104, 32), "b_f2": (136, 8),
              "Dp": (144, 8)}

        def bc(layer, name, chunk):
            off, n = BC[name]
            assert chunk < n
            return bcols_s[:, off + chunk:off + chunk + 1]

        CST_NEGC, CST_SCALE, CST_INVD, CST_EPS, CST_NHALF, CST_ONE = range(6)

        def cstc(i):
            return cst_s[:, i:i + 1]

        def emit_flag(ep):
            def fl(e, sems, ep=ep):
                e.dma_start(out=sh_fl[ep:ep + 1, DynSlice(get_par(e), 1)],
                            in_=fl_s[:, ep:ep + 1]).then_inc(sems["xw"], 16)
            s.raw("gpsimd", fl)
            s.cnt["xw"] += 16
            s.wait("gpsimd", "xw")

        def emit_poll(ep):
            def pl(e, sems, ep=ep):
                r = get_poll_regs(e)
                e.reg_mov(r["c"], 1)
                with e.While(r["c"]):
                    e.reg_add(r["w"], r["w"], 16)
                    e.dma_start(out=pf_s[:, 0:2],
                                in_=sh_fl[ep:ep + 1, :]).then_inc(sems["pp"], 16)
                    e.wait_ge(sems["pp"], r["w"])
                    e.load(r["v0"], pf_s[0:1, 0:1])
                    e.load(r["v1"], pf_s[0:1, 1:2])
                    e.load(r["vm"], fl_s[0:1, ep:ep + 1])
                    e.reg_sub(r["t0"], r["v0"], r["vm"])
                    e.reg_sub(r["t1"], r["v1"], r["vm"])
                    with e.If_eq(r["t0"], 0):
                        with e.If_eq(r["t1"], 0):
                            e.reg_mov(r["c"], 0)
            s.raw("gpsimd", pl)

        def preload():
            for dst, srca in [(onesb, ones_bf), (ones8, ones8f), (basis_s, basis),
                              (ident_s, identf), (cst_s, cst), (bxp_s, bxp),
                              (brr_s, br_rep), (fl_s, flagrows)]:
                s.op("sync", "dma_start", out=dst[:, :], in_=srca, incs=[("ld", 16)])
            s.op("sync", "dma_start", out=msk_s[64:128, :], in_=maskS, incs=[("ld", 16)])
            s.op("sync", "dma_start", out=wr_s[:, :, :],
                 in_=w_r.rearrange("(c p) e -> p c e", p=128), incs=[("ld", 16)])
            s.op("sync", "dma_start", out=a_d[:, :], in_=alog, incs=[("ld", 16)])
            s.op("gpsimd", "dma_start", out=b2r_s[:, :], in_=b2r, incs=[("wg", 16)])
            s.op("gpsimd", "dma_start", out=big0[:, :, :],
                 in_=xT.rearrange("(c p) t -> p c t", p=128), incs=[("wg", 16)])
            s.op("vector", "memset", sc_z[:, :], 0.0, incs=[("dv", 1)])
            s.op("vector", "memset", scmr[0:64, :], 1.0, incs=[("dv", 1)])
            s.wait("scalar", "ld")
            s.op("scalar", "activation", out=a_d[:, :], in_=a_d[:, :], func=AF.Exp,
                 incs=[("ev", 1)])
            s.wait("vector", "ev")
            s.op("vector", "tensor_scalar_mul", out=a_d[:, :], in0=a_d[:, :],
                 scalar1=-1.0 / D, incs=[("dv", 1)])
            s.barrier()

        def layernorm(layer, affine, router=False, spill_dst=None):
            for tt in range(NTT):
                t0 = tt * TT
                sq = big3[:, 0:8 * TT].rearrange("p (c t) -> p c t", t=TT)
                s.wait("tensor", "ev")
                for c in range(NCH):
                    s.op("tensor", "matmul", psC[:, 0:TT], onesb[:, :],
                         big0[:, c, t0:t0 + TT], start=(c == 0), stop=(c == NCH - 1),
                         incs=([("mm", 1)] if c == NCH - 1 else []))
                s.wait("scalar", "dv")
                for c in range(NCH):
                    s.op("scalar", "activation", out=sq[:, c, :],
                         in_=big0[:, c, t0:t0 + TT], func=AF.Square,
                         incs=[("ev", 1)] if c == NCH - 1 else [("sv", 1)])
                s.wait("tensor", "ev")
                s.wait("tensor", "sv", s.cnt["sv"])
                for c in range(NCH):
                    s.op("tensor", "matmul", psC[:, TT:2 * TT], onesb[:, :],
                         sq[:, c, :], start=(c == 0), stop=(c == NCH - 1),
                         incs=([("mm", 1)] if c == NCH - 1 else []))
                mu = f32p[:, 0:TT]
                var = f32p[:, TT:2 * TT]
                rstd = f32p[:, 2 * TT:3 * TT]
                tx = f32p[:, 3 * TT:4 * TT]
                if spill_dst is not None and tt > 0:
                    s.wait("vector", "sp")
                s.wait("vector", "mm")
                s.op("vector", "tensor_scalar_mul", out=mu, in0=psC[:, 0:TT],
                     scalar1=cstc(CST_INVD), incs=[("vv", 1)])
                s.op("vector", "tensor_scalar_mul", out=var, in0=psC[:, TT:2 * TT],
                     scalar1=cstc(CST_INVD), incs=[("dv", 1)])
                s.wait("vector", "dv")
                s.op("vector", "tensor_mul", out=tx, in0=mu, in1=mu, incs=[("vv", 1)])
                s.wait("vector", "vv")
                s.op("vector", "tensor_sub", out=var, in0=var, in1=tx,
                     incs=[("dv", 1)])
                s.wait("scalar", "dv")
                s.op("scalar", "activation", out=rstd, in_=var,
                     func=AF.Ln, bias=cstc(CST_EPS), incs=[("sv", 1)])
                s.wait("scalar", "sv")
                s.op("scalar", "activation", out=rstd, in_=rstd,
                     func=AF.Exp, scale=cstc(CST_NHALF), incs=[("ev", 2)])
                s.wait("vector", "ev")
                for c in range(NCH):
                    if router and c > 0:
                        s.wait("vector", "mm")
                    s.op("vector", "tensor_sub", out=tx,
                         in0=big0[:, c, t0:t0 + TT], in1=mu, incs=[("vv", 1)])
                    s.wait("vector", "vv")
                    s.op("vector", "tensor_mul", out=tx, in0=tx,
                         in1=rstd, incs=[("dv", 1)])
                    if router:
                        s.wait("tensor", "dv")
                        s.op("tensor", "matmul", psD[0:NEXP, 0:TT],
                             wr_s[:, c, :], tx[:, 0:TT],
                             start=(c == 0), stop=(c == NCH - 1),
                             incs=[("mm", 1)])
                    dst_ap = (big0[:, c, t0:t0 + TT] if spill_dst is None
                              else big3[:, 4096 + c * TT: 4096 + (c + 1) * TT])
                    s.wait("vector", "dv")
                    if affine is not None:
                        gn, bn = affine
                        s.op("vector", "tensor_scalar", out=dst_ap,
                             in0=tx, scalar1=bc(layer, gn, c),
                             scalar2=bc(layer, bn, c), op0=ALU.mult, op1=ALU.add,
                             incs=[("dv", 1)])
                    else:
                        if router:
                            s.wait("vector", "mm")
                        s.op("vector", "tensor_copy", out=dst_ap, in_=tx,
                             incs=[("dv", 1)])
                if router:
                    lgT = f32p[0:NEXP, 0:TT]
                    s.wait("vector", "mm")
                    s.op("vector", "tensor_copy", out=lgT, in_=psD[0:NEXP, 0:TT],
                         incs=[("dv", 1)])
                    for tg in range(TT // 128):
                        s.wait("tensor", "dv")
                        s.op("tensor", "transpose", psA[:, 0:NEXP],
                             lgT[:, tg * 128:(tg + 1) * 128], ident_s[0:NEXP, 0:NEXP],
                             incs=[("mm", 1)])
                        s.wait("vector", "mm")
                        s.op("vector", "tensor_add",
                             out=logit[:, tt * 4 + tg, :],
                             in0=psA[:, 0:NEXP],
                             in1=brr_s[:, :], incs=[("dv", 1)])
                if spill_dst is not None:
                    s.wait("gpsimd", "dv")
                    s.op("gpsimd", "dma_start",
                         out=spill_dst.rearrange("p (c t) -> p c t", t=TC)[:, :, t0:t0 + TT],
                         in_=big3[:, 4096:4096 + NCH * TT].rearrange("p (c t) -> p c t", t=TT),
                         incs=[("sp", 16)])
            s.barrier()

        def mamba_p1(layer, part="all"):
            winA = big1[:, 0:16384].rearrange("p (c o) -> p c o", o=DI)
            wdt = big1[:, 16384:32768].rearrange("p (c o) -> p c o", o=D)
            wxp = big3[:, 7168:8192].rearrange("p (c o) -> p c o", o=NST)
            s.op("gpsimd", "dma_start", out=winA[:, :, :],
                 in_=w_in[layer].rearrange("(c p) o -> p c o", p=128)[:, :, 0:DI],
                 incs=[("wg", 16)])
            s.op("gpsimd", "dma_start", out=wdt[:, :, :],
                 in_=w_dt[layer].rearrange("(c p) o -> p c o", p=128), incs=[("wg", 16)])
            s.op("gpsimd", "dma_start", out=wxp[:, :, :],
                 in_=w_xp[layer].rearrange("(c p) o -> p c o", p=128), incs=[("wg", 16)])
            s.op("sync", "dma_start", out=bcols_s[:, :],
                 in_=bcols[:, layer * 160:(layer + 1) * 160], incs=[("ld", 16)])
            s.op("vector", "memset", sc_st[0:64, 0:1], 0.0, incs=[("dv", 1)])
            s.op("vector", "memset", sc_st[0:64, 1:2], 1.0, incs=[("dv", 1)])
            s.wait("tensor", "wg")
            s.wait("vector", "ld")
            s.op("vector", "tensor_scalar_add", out=bcols_s[:, 144:152],
                 in0=bcols_s[:, 144:152], scalar1=1.0, incs=[("dv", 1)])
            s.wait("scalar", "ld")
            xi = big2[:, 0:8192].rearrange("p (c t) -> p c t", t=TT)
            spb = big3[:, 0:TT]
            for tt in range(NTT):
                t0 = tt * TT
                for oc in range(16):
                    ph = psA[:, (oc % 2) * TT:(oc % 2 + 1) * TT]
                    if tt or oc >= 2:
                        s.wait("tensor", "ev")
                    for c in range(NCH):
                        s.op("tensor", "matmul", ph, winA[:, c, oc * 128:(oc + 1) * 128],
                             big0[:, c, t0:t0 + TT], start=(c == 0), stop=(c == NCH - 1),
                             incs=([("mm", 1)] if c == NCH - 1 else []))
                    s.wait("scalar", "mm")
                    if tt > 0 and oc == 0:
                        s.wait("scalar", "sp")
                    s.op("scalar", "activation", out=xi[:, oc, :], in_=ph,
                         func=AF.Silu, bias=bc(layer, "b_in", oc), incs=[("ev", 1)])
                s.wait("gpsimd", "ev")
                s.op("gpsimd", "dma_start",
                     out=sp_xi.rearrange("p (n t) -> p n t", t=16 * TT)[:, tt, :],
                     in_=big2[:, 0:8192], incs=[("sp", 16)])
                dv_tile_prev = s.cnt["dv"]
                dv_spb = {}
                for dc in range(NCH):
                    ph = psB[:, (dc % 2) * TT:(dc % 2 + 1) * TT]
                    if dc >= 2:
                        s.wait("tensor", "dv", dv_spb[dc - 2])
                    elif tt > 0:
                        s.wait("tensor", "dv", dv_tile_prev)
                    for kc in range(16):
                        s.op("tensor", "matmul", ph, wdt[:, kc, dc * 128:(dc + 1) * 128],
                             xi[:, kc, :], start=(kc == 0), stop=(kc == 15),
                             incs=([("mm", 1)] if kc == 15 else []))
                    s.wait("scalar", "mm")
                    s.op("scalar", "activation", out=f32p[:, 0:TT], in_=ph, func=AF.Abs,
                         incs=[("sv", 1)])
                    s.wait("scalar", "sv")
                    s.op("scalar", "activation", out=f32p[:, TT:2 * TT],
                         in_=f32p[:, 0:TT], func=AF.Exp, scale=-1.0, incs=[("sv", 1)])
                    s.wait("scalar", "sv")
                    s.op("scalar", "activation", out=f32p[:, TT:2 * TT],
                         in_=f32p[:, TT:2 * TT], func=AF.Ln, bias=cstc(CST_ONE),
                         incs=[("ev", 1)])
                    s.wait("vector", "ev")
                    s.op("vector", "tensor_scalar_max", out=f32p[:, 2 * TT:3 * TT],
                         in0=ph, scalar1=0.0, incs=[("vv", 1)])
                    s.wait("vector", "vv")
                    s.op("vector", "tensor_add", out=spb,
                         in0=f32p[:, 2 * TT:3 * TT], in1=f32p[:, TT:2 * TT],
                         incs=[("dv", 1)])
                    dv_spb[dc] = s.cnt["dv"]
                    s.wait("tensor", "dv")
                    s.op("tensor", "matmul", psC[0:64, 0:TT], onesb[:, 0:64], spb,
                         start=(dc == 0), stop=(dc == NCH - 1),
                         incs=([("mm", 1)] if dc == NCH - 1 else []))
                s.wait("tensor", "ev")
                for kc in range(16):
                    s.op("tensor", "matmul", psD[0:64, 0:TT], wxp[:, kc, :],
                         xi[:, kc, :], start=(kc == 0), stop=(kc == 15),
                         incs=([("mm", 1)] if kc == 15 else []))
                s.wait("scalar", "mm")
                s.wait("scalar", "dv")
                s.op("scalar", "activation", out=sc_a[:, :],
                     in_=psC[0:64, 0:TT], func=AF.Exp,
                     scale=a_d[:, layer:layer + 1])
                s.op("scalar", "nop", incs=[("ev", 2)])
                s.wait("vector", "ev")
                s.op("vector", "tensor_scalar_add", out=sc_u[:, :], in0=psD[0:64, 0:TT],
                     scalar1=bxp_s[:, layer:layer + 1], incs=[("dv", 1)])
                s.wait("vector", "dv")
                s.op("vector", "tensor_tensor_scan", out=sc_t[:, :],
                     data0=sc_a[:, :], data1=sc_u[:, :],
                     initial=sc_st[0:64, 0:1], op0=ALU.mult, op1=ALU.add,
                     incs=[("vv", 1)])
                s.wait("vector", "vv")
                s.op("vector", "tensor_copy", out=big3[0:64, 512 + t0:512 + t0 + TT],
                     in_=sc_t[:, :])
                s.op("vector", "tensor_copy", out=sc_st[0:64, 0:1],
                     in_=sc_t[:, TT - 1:TT], incs=[("vv", 1)])
                s.wait("vector", "vv")
                s.op("vector", "tensor_tensor_scan", out=sc_t[:, :],
                     data0=sc_a[:, :], data1=sc_z[:, :],
                     initial=sc_st[0:64, 1:2], op0=ALU.mult, op1=ALU.add,
                     incs=[("vv", 1)])
                s.wait("vector", "vv")
                s.op("vector", "tensor_copy", out=big3[64:128, 512 + t0:512 + t0 + TT],
                     in_=sc_t[:, :])
                s.op("vector", "tensor_copy", out=sc_st[0:64, 1:2],
                     in_=sc_t[:, TT - 1:TT], incs=[("dv", 1)])
                s.wait("vector", "dv")
            if part == "tiles":
                s.barrier()
                return
            s.wait("gpsimd", "dv")

            def wr_sc(e, sems, layer=layer):
                e.dma_start(out=sh_sc[layer][DynSlice(get_par(e, "par64"), 64), :],
                            in_=sc_st[0:64, 0:2]).then_inc(sems["xw"], 16)
            s.raw("gpsimd", wr_sc)
            s.cnt["xw"] += 16
            s.wait("gpsimd", "xw")
            emit_flag(2 * layer)
            emit_poll(2 * layer)
            s.op("gpsimd", "dma_start", out=f32p[64:128, 0:2],
                 in_=sh_sc[layer][0:64, :], incs=[("wg", 16)])
            s.wait("vector", "wg")
            s.op("vector", "tensor_mul", out=sc_st[64:128, 2:3], in0=f32p[64:128, 0:1],
                 in1=msk_s[64:128, :], incs=[("vv", 1)])
            s.wait("vector", "vv")
            s.op("vector", "tensor_scalar_mul", out=scmr[64:128, :], in0=onesb[64:128, :],
                 scalar1=sc_st[64:128, 2:3], incs=[("dv", 1)])
            s.barrier()

        def mamba_p2(layer, part="all"):
            winB = big1[:, 0:16384].rearrange("p (c o) -> p c o", o=DI)
            wout = big1[:, 16384:32768].rearrange("p (c o) -> p c o", o=D)
            s.op("gpsimd", "dma_start", out=winB[:, :, :],
                 in_=w_in[layer].rearrange("(c p) o -> p c o", p=128)[:, :, DI:2 * DI],
                 incs=[("wg", 16)])
            s.op("gpsimd", "dma_start", out=wout[:, :, :],
                 in_=w_out[layer].rearrange("(c p) o -> p c o", p=128), incs=[("wg", 16)])
            s.wait("tensor", "wg")
            sres = big2[:, 0:8192].rearrange("p (c t) -> p c t", t=TT)
            xi = big2[:, 8192:16384].rearrange("p (c t) -> p c t", t=TT)
            for tt in range(NTT):
                t0 = tt * TT
                s.wait("gpsimd", "mm")
                s.op("gpsimd", "dma_start", out=big2[:, 8192:16384],
                     in_=sp_xi.rearrange("p (n t) -> p n t", t=16 * TT)[:, tt, :],
                     incs=[("wg", 16)])
                for oc in range(16):
                    ph = psA[:, (oc % 2) * TT:(oc % 2 + 1) * TT]
                    if tt or oc >= 2:
                        s.wait("tensor", "ev")
                    for c in range(NCH):
                        s.op("tensor", "matmul", ph, winB[:, c, oc * 128:(oc + 1) * 128],
                             big0[:, c, t0:t0 + TT], start=(c == 0), stop=(c == NCH - 1),
                             incs=([("mm", 1)] if c == NCH - 1 else []))
                    s.wait("scalar", "mm")
                    s.op("scalar", "activation", out=sres[:, oc, :], in_=ph,
                         func=AF.Silu, bias=bc(layer, "b_in", 16 + oc), incs=[("ev", 1)])
                s.wait("tensor", "dv")
                s.op("tensor", "matmul", psC[:, 0:TT], scmr[:, :],
                     big3[:, 512 + t0:512 + t0 + TT], start=True, stop=True,
                     incs=[("mm", 1)])
                s.wait("vector", "ev")
                s.wait("vector", "mm")
                s.wait("vector", "wg")
                for oc in range(16):
                    s.op("vector", "tensor_mul", out=big3[:, 0:TT], in0=xi[:, oc, :],
                         in1=sres[:, oc, :], incs=[("vv", 1)])
                    s.wait("vector", "vv")
                    s.op("vector", "tensor_mul", out=xi[:, oc, :], in0=big3[:, 0:TT],
                         in1=psC[:, 0:TT], incs=[("dv", 1)])
                if part == "b":
                    continue
                s.wait("tensor", "dv")
                for dc in range(NCH):
                    ph = psB[:, (dc % 2) * TT:(dc % 2 + 1) * TT]
                    if tt or dc >= 2:
                        s.wait("tensor", "dv")
                    for kc in range(16):
                        s.op("tensor", "matmul", ph, wout[:, kc, dc * 128:(dc + 1) * 128],
                             xi[:, kc, :], start=(kc == 0), stop=(kc == 15),
                             incs=([("mm", 1)] if kc == 15 else []))
                    s.wait("vector", "mm")
                    s.op("vector", "tensor_scalar_add", out=f32p[:, 0:TT], in0=ph,
                         scalar1=bc(layer, "b_out", dc), incs=[("dv", 1)])
                    s.wait("vector", "dv")
                    s.op("vector", "scalar_tensor_tensor",
                         out=big0[:, dc, t0:t0 + TT],
                         in0=big0[:, dc, t0:t0 + TT], scalar=bc(layer, "Dp", dc),
                         in1=f32p[:, 0:TT], op0=ALU.mult, op1=ALU.add,
                         incs=[("dv", 1)])
            s.barrier()

        def qkv(layer):
            wq = big1[:, 0:8192].rearrange("p (c o) -> p c o", o=D)
            wk = big1[:, 8192:16384].rearrange("p (c o) -> p c o", o=D)
            wv = big1[:, 16384:16384 + 8 * 1040].rearrange("p (c o) -> p c o", o=H * 65)
            s.op("gpsimd", "dma_start", out=wq[:, :, :],
                 in_=w_q[layer].rearrange("(c p) o -> p c o", p=128), incs=[("wg", 16)])
            s.op("gpsimd", "dma_start", out=wk[:, :, :],
                 in_=w_k[layer].rearrange("(c p) o -> p c o", p=128), incs=[("wg", 16)])
            s.op("gpsimd", "dma_start", out=wv[:, :, :],
                 in_=w_v[layer].rearrange("(c p) o -> p c o", p=128), incs=[("wg", 16)])
            s.op("gpsimd", "dma_start", out=bvrep_s[:, :], in_=bvrep[layer],
                 incs=[("wg", 16)])
            s.wait("tensor", "wg")
            qs = big2[:, 0:4096].rearrange("p (c t) -> p c t", t=TT)
            ks = big2[:, 4096:8192].rearrange("p (c t) -> p c t", t=TT)
            for tt in range(NTT):
                t0 = tt * TT
                if tt > 0:
                    s.wait("tensor", "sp")
                    s.wait("tensor", "xw")
                for oc in range(NCH):
                    phq = psA[:, (oc % 2) * TT:(oc % 2 + 1) * TT]
                    phk = psB[:, (oc % 2) * TT:(oc % 2 + 1) * TT]
                    if tt or oc >= 2:
                        s.wait("tensor", "ev")
                    for c in range(NCH):
                        s.op("tensor", "matmul", phq, wq[:, c, oc * 128:(oc + 1) * 128],
                             big0[:, c, t0:t0 + TT], start=(c == 0), stop=(c == NCH - 1))
                    for c in range(NCH):
                        s.op("tensor", "matmul", phk, wk[:, c, oc * 128:(oc + 1) * 128],
                             big0[:, c, t0:t0 + TT], start=(c == 0), stop=(c == NCH - 1),
                             incs=([("mm", 1)] if c == NCH - 1 else []))
                    s.wait("vector", "mm")
                    s.op("vector", "tensor_scalar_add", out=qs[:, oc, :], in0=phq,
                         scalar1=bc(layer, "b_q", oc))
                    s.op("vector", "tensor_scalar_add", out=ks[:, oc, :], in0=phk,
                         scalar1=bc(layer, "b_k", oc), incs=[("ev", 2)])
                for mg in range(TT // 128):
                    m0 = t0 + mg * 128
                    s.wait("tensor", "dv")
                    for c in range(NCH):
                        s.op("tensor", "matmul", psC[:, 0:512],
                             big0[:, c, m0:m0 + 128], wv[:, c, 0:512],
                             start=(c == 0), stop=(c == NCH - 1))
                    for c in range(NCH):
                        s.op("tensor", "matmul", psC[:, 512:1024],
                             big0[:, c, m0:m0 + 128], wv[:, c, 512:1024],
                             start=(c == 0), stop=(c == NCH - 1))
                    for c in range(NCH):
                        s.op("tensor", "matmul", psD[:, 0:16],
                             big0[:, c, m0:m0 + 128], wv[:, c, 1024:1040],
                             start=(c == 0), stop=(c == NCH - 1),
                             incs=([("mm", 1)] if c == NCH - 1 else []))
                    s.wait("vector", "mm")
                    s.wait("vector", "wg")
                    s.wait("vector", "xw")
                    s.op("vector", "tensor_add", out=big3[0:128, 0:1024],
                         in0=psC[:, 0:1024], in1=bvrep_s[:, 0:1024], incs=[("vv", 1)])
                    s.op("vector", "tensor_add", out=big3[:, 1024:1040],
                         in0=psD[:, 0:16], in1=bvrep_s[:, 1024:1040], incs=[("dv", 1)])
                    s.wait("gpsimd", "dv")
                    s.op("gpsimd", "dma_start",
                         out=lc_v.rearrange("(h t) f -> t h f", h=NCH)[m0:m0 + 128, :, :],
                         in_=big3[:, 0:1040].rearrange("p (h f) -> p h f", f=130),
                         incs=[("xw", 16)])
                s.wait("gpsimd", "ev")
                s.op("gpsimd", "dma_start",
                     out=sp_q.rearrange("(c p) t -> p c t", p=128)[:, :, t0:t0 + TT],
                     in_=qs[:, :, :], incs=[("sp", 16)])
                s.op("gpsimd", "dma_start",
                     out=lc_k.rearrange("(c p) t -> p c t", p=128)[:, :, t0:t0 + TT],
                     in_=ks[:, :, :], incs=[("xw", 16)])
            s.wait("gpsimd", "xw")

            def pub_kv(e, sems, layer=layer):
                e.dma_start(out=sh_k[layer][DynSlice(get_par(e, "parD"), D), :],
                            in_=lc_k).then_inc(sems["xw"], 16)
                e.dma_start(out=sh_v[layer][DynSlice(get_par(e, "parV"), NCH * TC), :],
                            in_=lc_v).then_inc(sems["xw"], 16)
            s.raw("gpsimd", pub_kv)
            s.cnt["xw"] += 32
            s.wait("gpsimd", "xw")
            emit_flag(2 * layer + 1)
            emit_poll(2 * layer + 1)
            s.barrier()

        def attention(layer):
            wao = big3[:, :].rearrange("p (c o) -> p c o", o=D)
            s.op("gpsimd", "dma_start", out=wao[:, :, :],
                 in_=w_ao[layer].rearrange("(c p) o -> p c o", p=128), incs=[("wg", 16)])
            NKC = S // 128
            hseq = 0
            qtg = [0]
            dv_cp = {}
            dv_mul = {}
            mm_after_hp = {}
            for qh in range(2):
                q0 = qh * 2048
                for hg in range(2):
                    s.wait("sync", "mm")
                    s.op("sync", "dma_start",
                         out=big2[:, 8192:16384].rearrange("p (i t) -> p i t", t=2048),
                         in_=sp_q.rearrange("(h p) t -> p h t", p=128)
                         [:, hg * 4:(hg + 1) * 4, q0:q0 + 2048],
                         incs=[("ld", 16)])
                    qg = big2[:, 8192:16384].rearrange("p (i t) -> p i t", t=2048)
                    s.wait("vector", "sp")
                    for hpi in range(4):
                        hp = hg * 4 + hpi
                        sl = hseq % 2
                        base = sl * 16512
                        kb = big1[:, base:base + 8192]
                        vb = big1[:, base + 8192:base + 16512].rearrange(
                            "p (r f) -> p r f", f=130)
                        if hseq >= 2:
                            s.wait("sync", "mm", mm_after_hp[hseq - 2])
                        for o in range(2):
                            s.op("sync", "dma_start",
                                 out=kb[:, o * 4096:(o + 1) * 4096],
                                 in_=sh_k[layer][o * D + hp * 128:o * D + (hp + 1) * 128, :],
                                 incs=[("ld", 16)])
                            s.op("sync", "dma_start",
                                 out=vb[:, o * 32:(o + 1) * 32, :],
                                 in_=sh_v[layer][(o * NCH + hp) * TC:(o * NCH + hp + 1) * TC, :]
                                 .rearrange("(r p) f -> p r f", p=128),
                                 incs=[("ld", 16)])
                        s.wait("tensor", "ld")
                        for qt in range(4):
                            qq = qt * TT
                            ah = (qt % 2) * TT
                            qv = qtg[0]
                            if qv >= 2:
                                s.wait("tensor", "dv", dv_mul[qv - 2])
                            if qv >= 1:
                                s.wait("tensor", "dv", dv_cp[qv - 1])

                            def emit_pv(kc):
                                pt = ptx[:, kc % 2, :]
                                s.wait("tensor", "ev", ev_exp[kc])
                                s.op("tensor", "matmul", psC[0:65, ah:ah + TT],
                                     vb[:, kc, 0:65], pt[:, 0:TT],
                                     start=(kc == 0), stop=(kc == NKC - 1))
                                s.op("tensor", "matmul", psD[0:65, ah:ah + TT],
                                     vb[:, kc, 65:130], pt[:, TT:2 * TT],
                                     start=(kc == 0), stop=(kc == NKC - 1),
                                     incs=[("mm", 1)])
                            ev_exp = {}
                            for kc in range(NKC):
                                ph = psA if kc % 2 == 0 else psB
                                if kc >= 2:
                                    s.wait("tensor", "ev", ev_exp[kc - 2])
                                s.op("tensor", "matmul", ph[:, 0:TT],
                                     kb[0:64, kc * 128:(kc + 1) * 128],
                                     qg[0:64, hpi, qq:qq + TT],
                                     start=True, stop=True)
                                s.op("tensor", "matmul", ph[:, TT:2 * TT],
                                     kb[64:128, kc * 128:(kc + 1) * 128],
                                     qg[64:128, hpi, qq:qq + TT],
                                     start=True, stop=True, incs=[("mm", 1)])
                                s.wait("scalar", "mm")
                                pt = ptx[:, kc % 2, :]
                                s.op("scalar", "activation", out=pt, in_=ph[:, :],
                                     func=AF.Exp, scale=cstc(CST_SCALE),
                                     bias=cstc(CST_NEGC), incs=[("ev", 1)])
                                ev_exp[kc] = s.cnt["ev"]
                                if kc >= 1:
                                    emit_pv(kc - 1)
                            emit_pv(NKC - 1)
                            s.wait("vector", "mm")
                            s.op("vector", "reciprocal", out=rcpb[0:1, 0:TT],
                                 in_=psC[64:65, ah:ah + TT])
                            s.op("vector", "reciprocal", out=rcpb[0:1, TT:2 * TT],
                                 in_=psD[64:65, ah:ah + TT], incs=[("dv", 1)])
                            s.wait("tensor", "dv")
                            s.op("tensor", "matmul", psA[:, 0:TT], onesb[0:1, :],
                                 rcpb[0:1, 0:TT], start=True, stop=True)
                            s.op("tensor", "matmul", psA[:, TT:2 * TT], onesb[0:1, :],
                                 rcpb[0:1, TT:2 * TT], start=True, stop=True,
                                 incs=[("mm", 1)])
                            s.wait("vector", "mm")
                            s.op("vector", "tensor_copy", out=f32p[:, 0:1024],
                                 in_=psA[:, :], incs=[("dv", 1)])
                            dv_cp[qtg[0]] = s.cnt["dv"]
                            s.wait("vector", "dv")
                            og = big2[:, 0:8192].rearrange("p (i t) -> p i t", t=2048)
                            s.op("vector", "tensor_mul",
                                 out=og[0:64, hpi, qq:qq + TT],
                                 in0=psC[0:64, ah:ah + TT], in1=f32p[0:64, 0:TT],
                                 incs=[("vv", 1)])
                            s.op("vector", "tensor_mul",
                                 out=og[64:128, hpi, qq:qq + TT],
                                 in0=psD[0:64, ah:ah + TT], in1=f32p[64:128, TT:2 * TT],
                                 incs=[("dv", 1)])
                            dv_mul[qtg[0]] = s.cnt["dv"]
                            qtg[0] += 1
                        mm_after_hp[hseq] = s.cnt["mm"]
                        hseq += 1
                    s.wait("gpsimd", "dv")
                    s.op("gpsimd", "dma_start",
                         out=sp_o.rearrange("p (h t) -> p h t", t=TC)
                         [:, hg * 4:(hg + 1) * 4, q0:q0 + 2048],
                         in_=big2[:, 0:8192].rearrange("p (i t) -> p i t", t=2048),
                         incs=[("sp", 16)])
            s.barrier()
            onv = big2[:, 0:4096].rearrange("p (h t) -> p h t", t=TT)
            for tt in range(NTT):
                t0 = tt * TT
                s.wait("sync", "mm")
                s.wait("sync", "sp")
                s.op("sync", "dma_start", out=onv[:, :, :],
                     in_=sp_o.rearrange("p (h t) -> p h t", t=TC)[:, :, t0:t0 + TT],
                     incs=[("ld", 16)])
                s.wait("tensor", "ld")
                for dc in range(NCH):
                    ph = psC[:, (dc % 2) * TT:(dc % 2 + 1) * TT]
                    if tt or dc >= 2:
                        s.wait("tensor", "dv")
                    for hp in range(NCH):
                        s.op("tensor", "matmul", ph,
                             wao[:, hp, dc * 128:(dc + 1) * 128],
                             onv[:, hp, :], start=(hp == 0), stop=(hp == NCH - 1),
                             incs=([("mm", 1)] if hp == NCH - 1 else []))
                    s.wait("vector", "mm")
                    s.op("vector", "tensor_scalar_add", out=f32p[:, 0:TT], in0=ph,
                         scalar1=bc(layer, "b_ao", dc), incs=[("vv", 1)])
                    s.wait("vector", "vv")
                    s.op("vector", "tensor_add",
                         out=big0[:, dc, t0:t0 + TT],
                         in0=f32p[:, 0:TT],
                         in1=big0[:, dc, t0:t0 + TT], incs=[("dv", 1)])
            s.barrier()

        def ffn(layer):
            s.op("gpsimd", "dma_start", out=sp_x,
                 in_=big0[:, :, :].rearrange("p c t -> p (c t)"), incs=[("sp", 16)])
            s.wait("gpsimd", "sp")
            wf1 = big0[:, :, :]
            wf2 = big1[:, 0:32768].rearrange("p (c o) -> p c o", o=D)
            s.op("gpsimd", "dma_start", out=big0[:, :, :],
                 in_=w_f1[layer].rearrange("(c p) o -> p c o", p=128), incs=[("wg", 16)])
            s.op("gpsimd", "dma_start", out=wf2[:, :, :],
                 in_=w_f2[layer].rearrange("(c p) o -> p c o", p=128), incs=[("wg", 16)])
            s.wait("tensor", "wg")
            xt = big3[:, 0:4096].rearrange("p (c t) -> p c t", t=TT)
            v3 = big3[:, 4096:8192].rearrange("p (c t) -> p c t", t=TT)
            hb = big2[:, :].rearrange("p (c t) -> p c t", t=TT)
            for tt in range(NTT):
                t0 = tt * TT
                s.wait("gpsimd", "mm")
                s.op("gpsimd", "dma_start", out=big3[:, 0:4096],
                     in_=sp_x.rearrange("p (c t) -> p c t", t=TC)[:, :, t0:t0 + TT],
                     incs=[("wg", 16)])
                s.wait("tensor", "wg")
                for fc in range(32):
                    ph = psA[:, (fc % 2) * TT:(fc % 2 + 1) * TT]
                    if tt or fc >= 2:
                        s.wait("tensor", "ev")
                    for c in range(NCH):
                        s.op("tensor", "matmul", ph, wf1[:, c, fc * 128:(fc + 1) * 128],
                             xt[:, c, :], start=(c == 0), stop=(c == NCH - 1),
                             incs=([("mm", 1)] if c == NCH - 1 else []))
                    s.wait("scalar", "mm")
                    s.op("scalar", "activation", out=hb[:, fc, :], in_=ph,
                         func=AF.Gelu, bias=bc(layer, "b_f1", fc), incs=[("ev", 1)])
                s.wait("tensor", "ev")
                s.wait("tensor", "dv")
                PS = [psA, psB, psC, psD]
                for fc in range(32):
                    for dc in range(NCH):
                        ph = PS[dc // 2][:, (dc % 2) * TT:(dc % 2 + 1) * TT]
                        s.op("tensor", "matmul", ph, wf2[:, fc, dc * 128:(dc + 1) * 128],
                             hb[:, fc, :], start=(fc == 0), stop=(fc == 31),
                             incs=([("mm", 1)] if (fc == 31 and dc == NCH - 1) else []))
                s.wait("vector", "mm")
                for dc in range(NCH):
                    ph = PS[dc // 2][:, (dc % 2) * TT:(dc % 2 + 1) * TT]
                    s.op("vector", "tensor_scalar_add", out=f32p[:, 0:TT], in0=ph,
                         scalar1=bc(layer, "b_f2", dc), incs=[("vv", 1)])
                    s.wait("vector", "vv")
                    s.op("vector", "tensor_add", out=v3[:, dc, :], in0=f32p[:, 0:TT],
                         in1=xt[:, dc, :], incs=[("dv", 1)])
                s.wait("gpsimd", "dv")
                s.op("gpsimd", "dma_start",
                     out=sp_x2.rearrange("p (c t) -> p c t", t=TC)[:, :, t0:t0 + TT],
                     in_=big3[:, 4096:8192].rearrange("p (c t) -> p c t", t=TT),
                     incs=[("sp", 16)])
            s.wait("gpsimd", "sp")
            s.op("gpsimd", "dma_start", out=big0[:, :, :].rearrange("p c t -> p (c t)"),
                 in_=sp_x2, incs=[("wg", 16)])
            s.barrier()

        def outer_end():
            for tt in range(NTT):
                t0 = tt * TT
                s.wait("gpsimd", "dv")
                s.op("gpsimd", "dma_start", out=big2[:, 0:4096],
                     in_=xT.rearrange("(c p) t -> p c t", p=128)[:, :, t0:t0 + TT],
                     incs=[("wg", 16)])
                s.wait("vector", "wg")
                x0t = big2[:, 0:4096].rearrange("p (c t) -> p c t", t=TT)
                for c in range(NCH):
                    s.op("vector", "tensor_add", out=big0[:, c, t0:t0 + TT],
                         in0=big0[:, c, t0:t0 + TT], in1=x0t[:, c, :],
                         incs=[("dv", 1)])
            s.barrier()
            layernorm(0, None, router=True, spill_dst=sp_z2)
            emit_masks()
            s.barrier()

        def emit_masks():
            for tg in range(32):
                if tg > 0:
                    s.wait("vector", "mm")
                s.op("vector", "reduce_max", out=mge(0, 1),
                     in_=logit[:, tg, :], axis=mybir.AxisListType.X, incs=[("vv", 1)])
                s.wait("vector", "vv")
                s.op("vector", "tensor_scalar", out=mge(1, NEXP),
                     in0=logit[:, tg, :], scalar1=mge(0, 1), scalar2=None,
                     op0=ALU.is_ge, incs=[("vv", 1)])
                s.wait("vector", "vv")
                s.op("vector", "tensor_tensor_scan", out=mge(2, NEXP),
                     data0=ones8[:, 0:NEXP], data1=mge(1, NEXP), initial=0.0,
                     op0=ALU.mult, op1=ALU.add, incs=[("vv", 1)])
                s.wait("vector", "vv")
                s.op("vector", "tensor_scalar", out=mge(2, NEXP),
                     in0=mge(2, NEXP), scalar1=1.0, scalar2=None,
                     op0=ALU.is_equal, incs=[("vv", 1)])
                s.wait("vector", "vv")
                s.op("vector", "tensor_mul", out=mge(1, NEXP),
                     in0=mge(1, NEXP), in1=mge(2, NEXP),
                     incs=[("dv", 1)])
                s.wait("tensor", "dv")
                s.wait("tensor", "dv")
                s.op("tensor", "transpose", psD[0:NEXP, 0:128],
                     mge(1, NEXP), ident_s[:, :], incs=[("mm", 1)])
                s.wait("vector", "mm")
                s.op("vector", "tensor_copy",
                     out=big3[0:NEXP, 4096 + tg * 128:4096 + (tg + 1) * 128],
                     in_=psD[0:NEXP, 0:128], incs=[("dv", 1)])

        def out_init():
            slot_ost = {0: 0, 1: 0}
            for tt in range(NTT):
                t0 = tt * TT
                for dc in range(NCH):
                    ph = psA[:, (dc % 2) * TT:(dc % 2 + 1) * TT]
                    if tt or dc >= 2:
                        s.wait("tensor", "dv")
                    s.op("tensor", "matmul", ph, b2r_s[:, dc * 128:(dc + 1) * 128],
                         big3[0:NEXP, 4096 + t0:4096 + t0 + TT], start=True, stop=True,
                         incs=[("mm", 1)])
                    s.wait("vector", "mm")
                    s.wait("vector", "ost", slot_ost[dc % 2])
                    s.op("vector", "tensor_add", out=stg(dc % 2),
                         in0=big0[:, dc, t0:t0 + TT], in1=ph, incs=[("dv", 1)])
                    s.wait("gpsimd", "dv")
                    s.op("gpsimd", "dma_start",
                         out=out.rearrange("(c p) t -> p c t", p=128)[:, dc, t0:t0 + TT],
                         in_=stg(dc % 2), incs=[("ost", 16)])
                    slot_ost[dc % 2] = s.cnt["ost"]
            s.barrier()

        def moe():
            slot_ost = {0: 0, 1: 0}
            w1b = big0[:, :, :]
            w2b = big1[:, 0:32768].rearrange("p (c o) -> p c o", o=D)
            zt = big3[:, 0:4096].rearrange("p (c t) -> p c t", t=TT)
            mb = f32p[:, 3 * TT:4 * TT]
            hb = big2[:, :].rearrange("p (c t) -> p c t", t=TT)
            for e in range(NEXP):
                s.wait("gpsimd", "mm")
                s.op("gpsimd", "dma_start", out=w1b[:, :, :],
                     in_=w1[e].rearrange("(c p) o -> p c o", p=128), incs=[("wg", 16)])
                s.op("gpsimd", "dma_start", out=w2b[:, :, :],
                     in_=w2[e].rearrange("(c p) o -> p c o", p=128), incs=[("wg", 16)])
                s.op("sync", "dma_start", out=b1c_s[:, :],
                     in_=b1c[:, e * 32:(e + 1) * 32], incs=[("ld", 16)])
                s.wait("scalar", "ld")
                ost_before = s.cnt["ost"]
                s.wait("tensor", "wg")
                for tt in range(NTT):
                    t0 = tt * TT
                    s.wait("gpsimd", "mm")
                    s.op("gpsimd", "dma_start", out=big3[:, 0:4096],
                         in_=sp_z2.rearrange("p (c t) -> p c t", t=TC)[:, :, t0:t0 + TT],
                         incs=[("wg", 16)])
                    s.wait("tensor", "dv")
                    s.op("tensor", "matmul", psD[:, 512:1024],
                         basis_s[:, e * 128:(e + 1) * 128],
                         big3[0:NEXP, 4096 + t0:4096 + t0 + TT],
                         start=True, stop=True, incs=[("mm", 1)])
                    s.wait("vector", "mm")
                    s.op("vector", "tensor_copy", out=mb, in_=psD[:, 512:1024],
                         incs=[("dv", 1)])
                    s.wait("tensor", "wg")
                    s.wait("tensor", "dv")
                    for fc in range(32):
                        ph = psA[:, (fc % 2) * TT:(fc % 2 + 1) * TT]
                        if tt or fc >= 2 or e:
                            s.wait("tensor", "ev")
                        for c in range(NCH):
                            s.op("tensor", "matmul", ph, w1b[:, c, fc * 128:(fc + 1) * 128],
                                 zt[:, c, :], start=(c == 0), stop=(c == NCH - 1),
                                 incs=([("mm", 1)] if c == NCH - 1 else []))
                        s.wait("scalar", "mm")
                        s.op("scalar", "activation", out=hb[:, fc, :], in_=ph,
                             func=AF.Relu, bias=b1c_s[:, fc:fc + 1],
                             incs=[("ev", 1)])
                    s.wait("tensor", "ev")
                    PS = [psA, psB, psC, psD]
                    for fc in range(32):
                        for dc in range(NCH):
                            ph = PS[dc // 2][:, (dc % 2) * TT:(dc % 2 + 1) * TT]
                            s.op("tensor", "matmul", ph, w2b[:, fc, dc * 128:(dc + 1) * 128],
                                 hb[:, fc, :], start=(fc == 0), stop=(fc == 31),
                                 incs=([("mm", 1)] if (fc == 31 and dc == NCH - 1) else []))
                    s.wait("vector", "mm")
                    for dc in range(NCH):
                        ph = PS[dc // 2][:, (dc % 2) * TT:(dc % 2 + 1) * TT]
                        s.wait("vector", "ost", slot_ost[dc % 2])
                        s.op("vector", "tensor_mul", out=stg(dc % 2), in0=ph,
                             in1=mb, incs=[("dv", 1)])
                        s.wait("gpsimd", "dv")
                        s.wait("gpsimd", "ost", ost_before)
                        s.op("gpsimd", "dma_start",
                             out=out.rearrange("(c p) t -> p c t", p=128)[:, dc, t0:t0 + TT],
                             in_=stg(dc % 2), accum_op=ALU.add, incs=[("ost", 16)])
                        slot_ost[dc % 2] = s.cnt["ost"]
            s.wait("gpsimd", "ost")
            s.barrier()

        def dump_stream():
            s.barrier()
            s.op("gpsimd", "dma_start",
                 out=out.rearrange("(c p) t -> p c t", p=128),
                 in_=big0[:, :, :], incs=[("ost", 16)])
            s.wait("gpsimd", "ost")
            s.barrier()

        phase_list = [("preload", preload),
                      ("ln0", lambda: layernorm(0, None)),
                      ("m1t0", lambda: mamba_p1(0, part="tiles")),
                      ("m1x0", lambda: mamba_p1(0)),
                      ("m2b0", lambda: (mamba_p1(0), mamba_p2(0, part="b")))]
        for l in range(L):
            phase_list += [
                (f"mamba{l}", lambda l=l: (mamba_p1(l), mamba_p2(l))),
                (f"ln1_{l}", lambda l=l: layernorm(l, ("g1", "be1"))),
                (f"qkv{l}", lambda l=l: qkv(l)),
                (f"attn{l}", lambda l=l: attention(l)),
                (f"ln2_{l}", lambda l=l: layernorm(l, ("g2", "be2"))),
                (f"ffn{l}", lambda l=l: ffn(l)),
            ]
        phase_list += [("outer", outer_end), ("outinit", out_init), ("moe", moe)]
        done = False
        for nm, fn in phase_list:
            SUB = ("m1t0", "m1x0", "m2b0")
            if upto not in SUB and nm in SUB:
                continue
            if upto in SUB:
                if nm in SUB and nm != upto:
                    continue
                if nm == "mamba0":
                    continue
            fn()
            if upto is not None and nm == upto:
                dump_stream()
                done = True
                break
        assert upto is None or done, f"unknown phase {upto}"

        lp = nc.allow_low_precision(reason="bf16 megakernel per problem tolerance")
        lp.__enter__()

        @block.sync
        def _(sync):
            for fn in s.prog["sync"]:
                fn(sync, sems)

        @block.gpsimd
        def _(gpsimd):
            for fn in s.prog["gpsimd"]:
                fn(gpsimd, sems)

        @block.tensor
        def _(tensor):
            for fn in s.prog["tensor"]:
                fn(tensor, sems)

        @block.scalar
        def _(scalar):
            for fn in s.prog["scalar"]:
                fn(scalar, sems)

        @block.vector
        def _(vector):
            for fn in s.prog["vector"]:
                fn(vector, sems)

        lp.__exit__(None, None, None)

    return nc



def _prep_maps(inputs, nonce):
    f32 = np.float32

    def T(a):
        return np.ascontiguousarray(np.asarray(a, f32).T)

    gi = {k: np.asarray(v, f32) for k, v in inputs.items()}
    x = gi["L0_attn_x"]

    w_in = np.stack([T(gi["L0_attn_W_in"][l]) for l in range(L)])
    w_dt = np.stack([T(gi["L0_attn_W_dt"][l]) for l in range(L)])
    w_xp = np.stack([T(gi["L0_attn_W_xp"][l]) for l in range(L)])
    w_out = np.stack([T(gi["L0_attn_W_out"][l]) for l in range(L)])
    wqkv = gi["L0_attn_W_qkv"]
    w_q = np.stack([T(wqkv[l, 0:D]) for l in range(L)])
    w_k = np.stack([T(wqkv[l, D:2 * D]) for l in range(L)])
    w_v = np.zeros((L, D, H * 65), f32)
    bvrep = np.zeros((L, 128, H * 65), f32)
    bqkv = gi["L0_attn_b_qkv"]
    for l in range(L):
        vT = T(wqkv[l, 2 * D:3 * D])
        for h in range(H):
            w_v[l, :, h * 65:h * 65 + 64] = vT[:, h * 64:(h + 1) * 64]
            bvrep[l, :, h * 65:h * 65 + 64] = bqkv[l, 2 * D + h * 64:2 * D + (h + 1) * 64][None, :]
            bvrep[l, :, h * 65 + 64] = 1.0
    w_ao = np.stack([T(gi["L0_attn_W_ao"][l]) for l in range(L)])
    w_f1 = np.stack([T(gi["L0_attn_W_f1"][l]) for l in range(L)])
    w_f2 = np.stack([T(gi["L0_attn_W_f2"][l]) for l in range(L)])

    def cols(v, n):
        return np.ascontiguousarray(np.asarray(v, f32).reshape(n, 128).T)

    bcols = np.zeros((128, L * 160), f32)
    for l in range(L):
        b0 = l * 160
        bcols[:, b0 + 0:b0 + 32] = cols(gi["L0_attn_b_in"][l], 32)
        bcols[:, b0 + 32:b0 + 40] = cols(gi["L0_attn_b_dt"][l], 8)
        bcols[:, b0 + 40:b0 + 48] = cols(gi["L0_attn_b_out"][l], 8)
        bcols[:, b0 + 48:b0 + 56] = cols(bqkv[l, 0:D], 8)
        bcols[:, b0 + 56:b0 + 64] = cols(bqkv[l, D:2 * D], 8)
        bcols[:, b0 + 64:b0 + 72] = cols(gi["L0_attn_b_ao"][l], 8)
        bcols[:, b0 + 72:b0 + 80] = cols(gi["L0_attn_g1"][l], 8)
        bcols[:, b0 + 80:b0 + 88] = cols(gi["L0_attn_be1"][l], 8)
        bcols[:, b0 + 88:b0 + 96] = cols(gi["L0_attn_g2"][l], 8)
        bcols[:, b0 + 96:b0 + 104] = cols(gi["L0_attn_be2"][l], 8)
        bcols[:, b0 + 104:b0 + 136] = cols(gi["L0_attn_b_f1"][l], 32)
        bcols[:, b0 + 136:b0 + 144] = cols(gi["L0_attn_b_f2"][l], 8)
        bcols[:, b0 + 144:b0 + 152] = cols(gi["L0_attn_Dp"][l], 8)
    bxp = np.ascontiguousarray(gi["L0_attn_b_xp"].T)
    alog = np.ascontiguousarray(gi["L0_attn_A_log"].T)
    w_r = T(gi["L0_ffn_Wr"])
    br_rep = np.broadcast_to(gi["L0_ffn_br"][None, :], (128, NEXP)).copy()
    w1 = np.stack([T(gi["L0_ffn_W1"][e]) for e in range(NEXP)])
    b1c = np.concatenate([cols(gi["L0_ffn_b1"][e], 32) for e in range(NEXP)], axis=1)
    w2 = np.stack([T(gi["L0_ffn_W2"][e]) for e in range(NEXP)])
    b2r = np.asarray(gi["L0_ffn_b2"], f32)

    ones_bf = np.ones((128, 128), ml_dtypes.bfloat16)
    ones8f = np.ones((128, NEXP), f32)
    identf = np.eye(128, dtype=f32)
    basis = np.zeros((NEXP, NEXP * 128), ml_dtypes.bfloat16)
    for e in range(NEXP):
        basis[e, e * 128:(e + 1) * 128] = 1.0
    cst = np.zeros((128, 8), f32)
    cst[:, 0] = -CSHIFT
    cst[:, 1] = SCALE
    cst[:, 2] = 1.0 / D
    cst[:, 3] = EPS
    cst[:, 4] = -0.5
    cst[:, 5] = 1.0
    flagrows = np.zeros((1, NEP), np.int32)
    for ep in range(NEP):
        flagrows[0, ep] = nonce + ep

    shared = dict(w_in=w_in, w_dt=w_dt, w_xp=w_xp, w_out=w_out, w_q=w_q, w_k=w_k,
                  w_v=w_v, w_ao=w_ao, w_f1=w_f1, w_f2=w_f2, bcols=bcols, bxp=bxp,
                  alog=alog, bvrep=bvrep, w_r=w_r, br_rep=br_rep,
                  w1=w1, b1c=b1c, w2=w2, b2r=b2r, ones_bf=ones_bf, ones8f=ones8f,
                  identf=identf, basis=basis, cst=cst, flagrows=flagrows)

    maps = []
    for c in range(8):
        cc = c % 4
        b, par = cc // 2, cc % 2
        xs = np.ascontiguousarray(x[b, par * TC:(par + 1) * TC, :].T)
        mS = np.zeros((64, 1), f32)
        if par == 1:
            mS[:, 0] = 1.0
        m = dict(shared)
        m["xT"] = xs
        m["maskS"] = mS
        maps.append(m)
    return maps


_CACHE = {}


def kernel(**inputs):
    if "nc" not in _CACHE:
        _CACHE["nc"] = build_program()
    nc = _CACHE["nc"]
    nonce = int(time.time() * 1000) % 1000000 * 100 + 13
    maps = _prep_maps(inputs, nonce)
    res = run_bass_kernel_spmd(nc, maps, core_ids=list(range(8)))
    halves = []
    for cc in range(4):
        o = np.asarray(res.results[cc]["out"])
        halves.append(o.T)
    outb0 = np.concatenate([halves[0], halves[1]], axis=0)
    outb1 = np.concatenate([halves[2], halves[3]], axis=0)
    return np.stack([outb0, outb1], axis=0).astype(np.float32)
